# version 08
# baseline (speedup 1.0000x reference)
import numpy as np
from contextlib import ExitStack

import concourse.bass as bass
import concourse.mybir as mybir
from concourse.bass_utils import run_bass_kernel_spmd

F32 = mybir.dt.float32
BF16 = mybir.dt.bfloat16
AF = mybir.ActivationFunctionType
ALU = mybir.AluOpType
AX = mybir.AxisListType.X

NCORES = 8
T = 1024
NT = 8
D = 2048
KC = 16
H = 16
HD = 128
NCH = 8
HE = 1408
NJ = 11
EPS = 1e-5

ENGS = ("pe", "dv", "ac", "po", "sy")
SEMS = ("pe", "dv", "ac", "po", "sy", "st", "dst", "slc", "dxs",
        "ild", "isp", "isa", "irl",
        "sw0", "sw1", "sw2", "sw3", "sw4", "wd0", "wd1", "swr", "swg", "swb",
        "xpr", "shb", "shs", "sf0", "sf1", "pl1", "pl2", "shr", "flc", "se1", "se2")


class Plan:
    def __init__(self):
        self.thunks = {e: [] for e in ENGS}
        self.cnt = {s: 0 for s in SEMS}
        self.waited = {(e, s): 0 for e in ENGS for s in SEMS}
        self.sems = {}

    def ev(self, sem):
        return (sem, self.cnt[sem])

    def em(self, eng, fn, inc=None, by=1):
        if eng == inc:
            self.wait(eng, (eng, self.cnt[eng]), force=True)
        elif eng == "po" and inc is not None:
            self.wait("po", ("po", self.cnt["po"]), force=True)
        if inc is not None:
            self.cnt[inc] += by
            incs, incv = inc, by
        else:
            incs, incv = None, 0
        sems = self.sems

        def thunk(e):
            ins = fn(e)
            if incs is not None:
                ins.then_inc(sems[incs], incv)

        self.thunks[eng].append(thunk)
        return self.ev(inc) if inc is not None else None

    def wait(self, eng, *evs, force=False):
        best = {}
        for ev in evs:
            if ev is None:
                continue
            (s, v) = ev
            if (s == eng and not force) or v <= 0:
                continue
            if v > self.waited[(eng, s)] and v > best.get(s, 0):
                best[s] = v
        needed = []
        for s, v in best.items():
            needed.append((s, v))
            self.waited[(eng, s)] = v
        if not needed:
            return
        sems = self.sems

        def thunk(e):
            for (s, v) in needed:
                e.wait_ge(sems[s], v)

        self.thunks[eng].append(thunk)


NPRE = 6144
NPG = 6
MAGIC = 1234.5


def build(upto="full", dumps=(), prefix=False, detect=True):
    nc = bass.Bass("TRN2", target_bir_lowering=False,
                   detect_race_conditions=detect)

    def din(name, shape, dtype=F32):
        return nc.dram_tensor(name, list(shape), dtype, kind="ExternalInput").ap()

    x_in = din("x", (T, D))
    ldec = din("ldec", (1, H))
    pce = din("pce", (1, NCORES))
    nw_in = din("nw", (1, D))
    rw_in = din("rw", (8, D))
    wq_in = din("wq", (D, D)); wk_in = din("wk", (D, D))
    wv_in = din("wv", (D, D)); wo_in = din("wo", (D, D))
    wg1_in = din("wg1", (D, HD)); wg2_in = din("wg2", (HD, D))
    wgE = din("wgE", (8, D, HE)); wuE = din("wuE", (8, D, HE))
    wdE = din("wdE", (8, HE, D))
    sg_in = din("sg", (D, 2 * HE)); su_in = din("su", (D, 2 * HE))
    sd_in = din("sd", (2 * HE, D))
    s_all = din("s_all", (NCORES * HD, D))
    if prefix:
        xpre_in = din("xpre", (NPRE, D))
        wkh_in = din("wkh", (D, 1024))
        wvh_in = din("wvh", (D, 1024))
        ldh_in = din("ldh", (1, 8))
        mg_in = din("mg", (1, 3))
        slot_in = din("slot", (1, 1), mybir.dt.int32)
        pslot_in = din("pslot", (1, 1), mybir.dt.int32)
        sh = nc.dram_tensor("sh", [2 * 128, 3072], F32, addr_space="Shared").ap()
        shf = nc.dram_tensor("shf", [2, 16], F32, addr_space="Shared").ap()

    y_out = nc.dram_tensor("y", [T, D], F32, kind="ExternalOutput").ap()
    sloc_out = nc.dram_tensor("sloc", [HD, D], F32, kind="ExternalOutput").ap()
    xspill = nc.dram_tensor("xspill", [T, D], F32).ap()

    dump_t = {}
    for nm in dumps:
        dump_t[nm] = nc.dram_tensor(nm, [128, 16384], F32, kind="ExternalOutput").ap()
    n_dumps = len(dump_t)

    with ExitStack() as ctx:
        e_ = ctx.enter_context
        A = e_(nc.sbuf_tensor("A", [128, 16384], F32))
        B = e_(nc.sbuf_tensor("B", [128, 16384], BF16))
        C = e_(nc.sbuf_tensor("C", [128, 16384], BF16))
        W = e_(nc.sbuf_tensor("W", [128, 3 * 2048], BF16))
        SR = e_(nc.sbuf_tensor("SR", [128, 6144], F32))
        SB16 = e_(nc.sbuf_tensor("sbf", [128, 2048], BF16))
        LS = e_(nc.sbuf_tensor("lnscr", [128, 2048], BF16))
        VC = e_(nc.sbuf_tensor("vch", [128, 2048], BF16))
        KCH = e_(nc.sbuf_tensor("kpch", [128, 2048], BF16))
        DT = e_(nc.sbuf_tensor("DTt", [128, 2048], BF16))
        QD = e_(nc.sbuf_tensor("QDt", [128, 2048], BF16))
        NW = e_(nc.sbuf_tensor("NWb", [128, 2048], BF16))
        WG2 = e_(nc.sbuf_tensor("WG2s", [128, 2048], BF16))
        G1 = e_(nc.sbuf_tensor("G1f", [128, 1024], BF16))
        LDB = e_(nc.sbuf_tensor("LDB", [128, 16], F32))
        KDC = e_(nc.sbuf_tensor("KDC", [128, 16], F32))
        CDB = e_(nc.sbuf_tensor("CDB", [128, 16], F32))
        PCEb = e_(nc.sbuf_tensor("PCEb", [128, 8], F32))
        COEF = e_(nc.sbuf_tensor("COEF", [128, 128], F32))
        STT = e_(nc.sbuf_tensor("STTb", [128, 128], F32))
        RSTD = e_(nc.sbuf_tensor("RSTDb", [128, 128], F32))
        RT = e_(nc.sbuf_tensor("RTb", [128, 256], F32))
        G = e_(nc.sbuf_tensor("Gb", [128, 64], F32))
        RWT = e_(nc.sbuf_tensor("RWTb", [128, 128], BF16))
        IDT = e_(nc.sbuf_tensor("IDTb", [128, 128], BF16))
        ONE = e_(nc.sbuf_tensor("ONEb", [1, 128], F32))
        MS = e_(nc.sbuf_tensor("MSb", [128, 128], F32))
        KDF = e_(nc.sbuf_tensor("KDFb", [128, 2048], BF16))
        EPSC = e_(nc.sbuf_tensor("EPSC", [128, 1], F32))
        SLT = e_(nc.sbuf_tensor("SLT", [1, 2], mybir.dt.int32))
        FLG = e_(nc.sbuf_tensor("FLG", [1, 2], F32))

        PA = e_(nc.psum_tensor("PA", [128, 1024], F32))
        PB = e_(nc.psum_tensor("PB", [128, 1024], F32))
        PC = e_(nc.psum_tensor("PC", [128, 2048], F32))

        P = Plan()
        for s in SEMS:
            P.sems[s] = e_(nc.semaphore(s))

        Ab = A[:, :].bitcast(BF16)
        PCb = PC[:, :].bitcast(BF16)
        SRb = SR[:, :].bitcast(BF16)
        B3 = B[:, :].rearrange("p (k n) -> p k n", k=KC)
        C3 = C[:, :].rearrange("p (k n) -> p k n", k=KC)

        def xs(t):
            return A[:, t * 2048:(t + 1) * 2048]

        def qf(h, sl=None):
            ap = Ab[:, h * 1024:(h + 1) * 1024]
            return ap if sl is None else ap[:, sl]

        def kf(h, sl=None):
            ap = Ab[:, 16384 + h * 1024:16384 + (h + 1) * 1024]
            return ap if sl is None else ap[:, sl]

        def vf(h, sl=None):
            ap = C[:, h * 1024:(h + 1) * 1024]
            return ap if sl is None else ap[:, sl]

        def fmb(k, sl=None):
            ap = B[:, k * 1024:(k + 1) * 1024]
            return ap if sl is None else ap[:, sl]

        def ob(t):
            return B[:, t * 2048:(t + 1) * 2048]

        Ssb = SR[:, 0:2048]
        SIb = SR[:, 2048:4096]
        SJb = SR[:, 4096:6144]
        SCB = SJb.bitcast(BF16)
        TMPF = SR[:, 0:2048]

        wslot = [W[:, s * 2048:(s + 1) * 2048] for s in range(3)] + \
                [C[:, 11264:13312], C[:, 13312:15360]]
        ring_n = [3]

        def emit_x_load(sem):
            return P.em("sy", lambda e: e.dma_start(
                out=A[:, :].rearrange("p (t d) -> p t d", t=NT),
                in_=x_in.rearrange("(t p) d -> p t d", p=128)), inc=sem, by=16)

        if not prefix:
            emit_x_load("ild")
        P.em("sy", lambda e: e.dma_start(out=MS[0:1, 0:16], in_=ldec), inc="ild", by=16)
        P.em("sy", lambda e: e.dma_start(out=MS[0:1, 16:24], in_=pce),
             inc="ild", by=16)
        P.em("sy", lambda e: e.dma_start(out=TMPF[0:1, :], in_=nw_in),
             inc="ild", by=16)
        ILD = ("ild", 64 if not prefix else 80)
        ev_x = ILD
        ev_sm = ILD
        ev_nws = ILD
        RWS = LS
        ev_rw = P.em("po", lambda e: e.dma_start(out=RWS[0:8, :], in_=rw_in),
                     inc="swr", by=16)
        ev_wg2 = P.em("po", lambda e: e.dma_start(
            out=WG2[:, :].rearrange("p (g d) -> p g d", g=1),
            in_=wg2_in.rearrange("(g p) d -> p g d", p=128)), inc="swg", by=16)

        ev_one = P.em("dv", lambda e: e.memset(ONE[:, :], 1.0), inc="dv")
        ev_eps = P.em("dv", lambda e: e.memset(EPSC[:, :], EPS), inc="dv")
        ev_ms = P.em("po", lambda e: e.memset(IDT[:, :], 1.0), inc="po")
        P.wait("po", ev_ms, force=True)
        ev_idt = P.em("po", lambda e: e.affine_select(
            out=IDT[:, :], in_=IDT[:, :], pattern=[[1, 128]],
            compare_op=ALU.is_equal, fill=0.0, base=0, channel_multiplier=-1),
            inc="po")
        ev_io = P.em("po", lambda e: e.iota(
            COEF[:, :], pattern=[[1, 128]], base=0, channel_multiplier=-1,
            allow_small_or_imprecise_dtypes=True), inc="po")
        P.wait("po", ev_io, force=True)
        ev_td = P.em("po", lambda e: e.affine_select(
            out=COEF[:, :], in_=COEF[:, :], pattern=[[1, 128]],
            compare_op=ALU.is_ge, fill=1e9, base=0, channel_multiplier=-1),
            inc="po")
        ev_ti = P.em("po", lambda e: e.iota(
            RSTD[:, :], pattern=[[1, 128]], base=1, channel_multiplier=0,
            allow_small_or_imprecise_dtypes=True), inc="po")
        ev_tk = P.em("po", lambda e: e.iota(
            STT[:, 0:16], pattern=[[0, 16]], base=127, channel_multiplier=-1,
            allow_small_or_imprecise_dtypes=True), inc="po")

        P.wait("pe", ev_sm, ev_one)
        ev_b1 = P.em("pe", lambda e: e.matmul(PA[:, 0:16], ONE[0:1, :], MS[0:1, 0:16],
                                              start=True, stop=True))
        ev_b2 = P.em("pe", lambda e: e.matmul(PA[:, 16:24], ONE[0:1, :],
                                              MS[0:1, 16:24], start=True, stop=True),
                     inc="pe")
        P.wait("dv", ev_b2)
        ev_ldb = P.em("dv", lambda e: e.tensor_copy(LDB[:, :], PA[:, 0:16]), inc="dv")
        ev_pce = P.em("dv", lambda e: e.tensor_copy(PCEb[:, :], PA[:, 16:24]), inc="dv")

        P.wait("ac", ev_td, ev_ldb)
        for h in range(H):
            P.em("ac", lambda e, h=h: e.activation(
                DT[:, h * 128:(h + 1) * 128], COEF[:, :], AF.Exp,
                scale=LDB[:, h:h + 1]), inc="ac")
        ev_dt = P.ev("ac")
        def build_qd(extra_wait=None, redo_iota=False):
            if redo_iota:
                P.wait("po", extra_wait, ("ac", P.cnt["ac"]))
                ev_ti2 = P.em("po", lambda e: e.iota(
                    RSTD[:, :], pattern=[[1, 128]], base=1, channel_multiplier=0,
                    allow_small_or_imprecise_dtypes=True), inc="po")
                P.wait("ac", ev_ti2, extra_wait)
            else:
                P.wait("ac", ev_ti, extra_wait)
            for h in range(H):
                P.em("ac", lambda e, h=h: e.activation(
                    QD[:, h * 128:(h + 1) * 128], RSTD[:, :], AF.Exp,
                    scale=LDB[:, h:h + 1]), inc="ac")
            return P.ev("ac")

        if not prefix:
            ev_qd = build_qd()
        else:
            ev_qd = None
        P.wait("dv", ev_tk, ev_ldb)
        ev_kd0 = P.em("dv", lambda e: e.tensor_tensor(
            STT[:, 16:32], STT[:, 0:16], LDB[:, :], ALU.mult), inc="dv")
        P.wait("ac", ev_kd0)
        ev_kdc = P.em("ac", lambda e: e.activation(KDC[:, :], STT[:, 16:32], AF.Exp),
                      inc="ac")
        P.wait("po", ev_qd if ev_qd is not None else ("ac", P.cnt["ac"]))
        ev_tk2 = P.em("po", lambda e: e.iota(
            RSTD[:, :], pattern=[[0, 128]], base=127, channel_multiplier=-1,
            allow_small_or_imprecise_dtypes=True), inc="po")
        P.wait("ac", ev_tk2)
        for h in range(H):
            P.em("ac", lambda e, h=h: e.activation(
                KDF[:, h * 128:(h + 1) * 128], RSTD[:, :], AF.Exp,
                scale=LDB[:, h:h + 1]), inc="ac")
        ev_kdf = P.ev("ac")
        ev_cdb = P.em("ac", lambda e: e.activation(CDB[:, :], LDB[:, :], AF.Exp,
                                                   scale=128.0), inc="ac")
        if not prefix:
            P.wait("dv", ev_pce, ev_qd, ev_dt)
            for j in range(NCORES):
                P.em("dv", lambda e, j=j: e.tensor_scalar(
                    COEF[:, j * 16:(j + 1) * 16], LDB[:, :], PCEb[:, j:j + 1],
                    None, ALU.mult), inc="dv")
            ev_cf0 = P.ev("dv")
            P.wait("ac", ev_cf0)
            ev_coef = P.em("ac", lambda e: e.activation(
                COEF[:, :], COEF[:, :], AF.Exp), inc="ac")
        else:
            P.em("sy", lambda e: e.dma_start(out=MS[0:1, 24:32], in_=ldh_in),
                 inc="ild", by=16)
            P.em("sy", lambda e: e.dma_start(out=MS[0:1, 28:31][:, 0:0]
                                             if False else MS[0:1, 96:99],
                                             in_=mg_in), inc="ild", by=16)
            P.em("sy", lambda e: e.dma_start(out=SLT[0:1, 0:1], in_=slot_in),
                 inc="shr", by=16)
            P.em("sy", lambda e: e.dma_start(out=SLT[0:1, 1:2], in_=pslot_in),
                 inc="shr", by=16)
            P.wait("pe", ("ild", 80))
            P.em("pe", lambda e: e.matmul(PB[:, 0:8], ONE[0:1, :], MS[0:1, 24:32],
                                          start=True, stop=True))
            ev_bc3 = P.em("pe", lambda e: e.matmul(
                PB[:, 8:11], ONE[0:1, :], MS[0:1, 96:99],
                start=True, stop=True), inc="pe")
            P.wait("dv", ev_bc3)
            ev_ldh = P.em("dv", lambda e: e.tensor_copy(MS[:, 96:104], PB[:, 0:8]),
                          inc="dv")
            ev_mgb = P.em("dv", lambda e: e.tensor_copy(MS[:, 120:123], PB[:, 8:11]),
                          inc="dv")
            P.wait("dv", ev_tk)
            ev_kp0 = P.em("dv", lambda e: e.tensor_tensor(
                MS[:, 104:112], STT[:, 0:8], MS[:, 96:104], ALU.mult), inc="dv")
            P.wait("ac", ev_kp0)
            ev_kpre = P.em("ac", lambda e: e.activation(
                MS[:, 104:112], MS[:, 104:112], AF.Exp), inc="ac")
            ev_cpre = P.em("ac", lambda e: e.activation(
                MS[:, 112:120], MS[:, 96:104], AF.Exp, scale=128.0), inc="ac")
            P.wait("ac", ev_tk2)
            for h in range(8):
                P.em("ac", lambda e, h=h: e.activation(
                    QD[:, h * 128:(h + 1) * 128], RSTD[:, :], AF.Exp,
                    scale=MS[:, 96 + h:97 + h]), inc="ac")
            ev_kdfpre = P.ev("ac")
            P.wait("dv", ev_dt, ev_mgb)
            for j in range(3):
                P.em("dv", lambda e, j=j: e.tensor_scalar(
                    COEF[:, j * 16:(j + 1) * 16], LDB[:, :],
                    MS[:, 120 + j:121 + j], None, ALU.mult), inc="dv")
            ev_cf0 = P.ev("dv")
            P.wait("ac", ev_cf0)
            ev_coef = P.em("ac", lambda e: e.activation(
                COEF[:, 0:48], COEF[:, 0:48], AF.Exp), inc="ac")
            P.wait("dv", ev_coef)
            ev_c1n = P.em("dv", lambda e: e.tensor_scalar(
                COEF[:, 48:64], COEF[:, 32:48], -1.0, None, ALU.mult), inc="dv")

        P.wait("pe", ev_nws)
        for c4 in range(4):
            ev_nwb = P.em("pe", lambda e, c4=c4: e.matmul(
                PC[:, c4 * 512:(c4 + 1) * 512], ONE[0:1, :],
                TMPF[0:1, c4 * 512:(c4 + 1) * 512], start=True, stop=True),
                inc="pe" if c4 == 3 else None)
        P.wait("dv", ev_nwb)
        ev_nw2 = P.em("dv", lambda e: e.tensor_copy(NW[:, :], PC[:, :]), inc="dv")

        P.wait("pe", ev_rw, ev_idt, ev_nw2)
        for k in range(KC):
            ev_rt0 = P.em("pe", lambda e, k=k: e.transpose(
                PCb[:, k * 8:(k + 1) * 8], RWS[0:8, k * 128:(k + 1) * 128],
                IDT[0:8, 0:8]), inc="pe" if k == KC - 1 else None)
        P.wait("dv", ev_rt0)
        ev_rwt = P.em("dv", lambda e: e.tensor_copy(RWT[:, :], PCb[:, 0:128]),
                      inc="dv")

        def ln_stats_maker(src_fn, sb, extra_ac_wait=None, ac_wait=None,
                           dv_wait=None):
            ev_sq = {}
            ev_rstd_t = {}

            def step(t):
                if t < NT:
                    if t == 0:
                        P.wait("ac", extra_ac_wait, ac_wait, dv_wait)
                    src = src_fn(t)
                    P.em("ac", lambda e, src=src, t=t: e.activation(
                        TMPF[:, :], src, AF.Copy,
                        accum_out=sb[:, 32 + t:33 + t]), inc="ac")
                    ev_sq[t] = P.em("ac", lambda e, src=src, t=t: e.activation(
                        TMPF[:, :], src, AF.Square,
                        accum_out=sb[:, 40 + t:41 + t]), inc="ac")
                tp = t - 1
                if 0 <= tp < NT:
                    P.wait("dv", ev_sq[tp])
                    P.em("dv", lambda e, t=tp: e.tensor_scalar(
                        sb[:, 48 + t:49 + t], sb[:, 32 + t:33 + t], -1.0 / D,
                        None, ALU.mult), inc="dv")
                    P.em("dv", lambda e, t=tp: e.tensor_tensor(
                        sb[:, 56 + t:57 + t], sb[:, 48 + t:49 + t],
                        sb[:, 48 + t:49 + t], ALU.mult), inc="dv")
                    ev_var = P.em("dv", lambda e, t=tp: e.scalar_tensor_tensor(
                        sb[:, 64 + t:65 + t], sb[:, 40 + t:41 + t], 1.0 / D,
                        sb[:, 56 + t:57 + t], ALU.mult, ALU.subtract),
                        inc="dv")
                    P.wait("ac", ev_var)
                    P.em("ac", lambda e, t=tp: e.activation(
                        sb[:, 72 + t:73 + t], sb[:, 64 + t:65 + t], AF.Ln,
                        bias=EPSC[:, 0:1]), inc="ac")
                    ev_rstd_t[tp] = P.em("ac", lambda e, t=tp: e.activation(
                        sb[:, 80 + t:81 + t], sb[:, 72 + t:73 + t], AF.Exp,
                        scale=-0.5), inc="ac")
            return step, ev_rstd_t

        def ln_apply_maker(src_fn, dst3, sb, ev_rstd_t, apply_wait=None):
            read_done = {}
            tr_evs = {}
            fm_evs = {}

            def astep(tq):
                if not (0 <= tq < NT):
                    return
                src = src_fn(tq)
                buf = LS if tq % 2 == 0 else VC
                half = (tq % 2) * 2048
                P.wait("dv", ev_rstd_t[tq], tr_evs.get(tq - 2),
                       apply_wait if tq <= 1 else None)
                ev_cp = P.em("dv", lambda e, src=src, t=tq, buf=buf:
                             e.tensor_scalar(
                                 buf[:, :], src, sb[:, 48 + t:49 + t],
                                 sb[:, 80 + t:81 + t], ALU.add, ALU.mult),
                             inc="dv")
                read_done[tq] = ev_cp
                P.wait("pe", ev_cp, fm_evs.get(tq - 2))
                for k in range(KC):
                    ev_tr = P.em("pe", lambda e, k=k, buf=buf, half=half:
                                 e.transpose(
                                     PCb[:, half + k * 128:half + (k + 1) * 128],
                                     buf[:, k * 128:(k + 1) * 128], IDT[:, :]),
                                 inc="pe" if k == KC - 1 else None)
                tr_evs[tq] = ev_tr
                P.wait("ac", ev_tr)
                fm_evs[tq] = P.em("ac", lambda e, tq=tq, half=half: e.copy(
                    dst3[:, :, tq * 128:(tq + 1) * 128],
                    PCb[:, half:half + 2048].rearrange("p (k n) -> p k n", k=KC)),
                    inc="ac")
            return astep, read_done, fm_evs

        def ln_apply(src_fn, dst3, sb, ev_rstd_t, apply_wait=None):
            astep, read_done, fm_evs = ln_apply_maker(
                src_fn, dst3, sb, ev_rstd_t, apply_wait=apply_wait)
            for tq in range(NT):
                astep(tq)
            return [read_done[t] for t in range(NT)], fm_evs[NT - 1]

        def emit_ln(src_fn, dst3, extra_ac_wait=None, sb=None, apply_wait=None,
                    ac_wait=None):
            sbu = MS if sb is None else sb
            step, ev_rstd_t = ln_stats_maker(src_fn, sbu,
                                             extra_ac_wait=extra_ac_wait,
                                             ac_wait=ac_wait)
            astep, read_done, fm_evs = ln_apply_maker(
                src_fn, dst3, sbu, ev_rstd_t, apply_wait=apply_wait)
            for t in range(NT + 3):
                if t <= NT:
                    step(t)
                astep(t - 2)
            return [read_done[t] for t in range(NT)], fm_evs[NT - 1]

        ring_owner = [None, None, None, None, None]
        slot_ctr = [0]

        def load_wslot(src3, waits=()):
            s = slot_ctr[0] % ring_n[0]
            slot_ctr[0] += 1
            P.wait("po", ring_owner[s], *waits)
            ev = P.em("po", lambda e, s=s, src3=src3: e.dma_start(
                out=wslot[s][:, :].rearrange("p (a b) -> p a b", a=src3.shape[1]),
                in_=src3), inc=f"sw{s}", by=16)
            return s, ev

        def eblk(w_ap, e0, ew=128):
            return w_ap.rearrange("(k p) e -> p k e", p=128)[:, :, e0:e0 + ew]

        proj_war = {0: ev_pce, 1: ev_pce}

        def proj(w_ap, dst_fn, act, n_eb=KC):
            for ebi in range(n_eb):
                s, ev_l = load_wslot(eblk(w_ap, ebi * 128))
                par = ebi % 2
                ps = PA if par == 0 else PB
                P.wait("pe", ev_l, proj_war[par])
                for k in range(KC):
                    P.em("pe", lambda e, s=s, k=k, ps=ps: e.matmul(
                        ps[:, 0:512], wslot[s][:, k * 128:(k + 1) * 128],
                        fmb(k, slice(0, 512)),
                        start=(k == 0), stop=(k == KC - 1)))
                    ev_mm = P.em("pe", lambda e, s=s, k=k, ps=ps: e.matmul(
                        ps[:, 512:1024], wslot[s][:, k * 128:(k + 1) * 128],
                        fmb(k, slice(512, 1024)),
                        start=(k == 0), stop=(k == KC - 1)),
                        inc="pe" if k == KC - 1 else None)
                ring_owner[s] = ev_mm
                P.wait("ac", ev_mm)
                dst = dst_fn(ebi)
                if act is None:
                    ev_e = P.em("ac", lambda e, ps=ps, dst=dst: e.copy(dst, ps[:, :]),
                                inc="ac")
                else:
                    ev_e = P.em("ac", lambda e, ps=ps, dst=dst, act=act: e.activation(
                        dst, ps[:, :], act), inc="ac")
                proj_war[par] = ev_e
            return P.ev("ac")

        if prefix:
            xp3 = xpre_in.rearrange("(g t p) d -> p g t d", p=128, t=NT)
            ev_xg = {}
            ev_xg[0] = P.em("sy", lambda e: e.dma_start(
                out=A[:, :].rearrange("p (t d) -> p t d", t=NT),
                in_=xp3[:, 0]), inc="xpr", by=16)
            P.wait("po", ("shr", 32))
            P.em("po", lambda e: e.memset(FLG[:, :], 0.0), inc="po")
            rs = None
            rp = None

            def load_rs(e):
                return e.value_load(SLT[0:1, 0:1])

            def load_rp(e):
                return e.value_load(SLT[0:1, 1:2])

            def th_clear(e):
                r = load_rs(e)
                return e.dma_start(out=shf[bass.ts(r, 1), 0:2], in_=FLG[0:1, 0:2])
            P.em("po", th_clear, inc="flc", by=16)
            P.wait("po", ("flc", 16))
            P.em("po", lambda e: e.memset(FLG[0:1, 1:2], MAGIC), inc="po")

            ev_bh0 = P.em("dv", lambda e: e.memset(SJb[:, 0:1024], 0.0), inc="dv")
            ev_bh = ev_bh0
            ev_pvc = None
            last_buf_apply = {0: None, 1: None}
            step0, rstd0 = ln_stats_maker(
                xs, MS, extra_ac_wait=ev_nwb, ac_wait=ev_xg[0],
                dv_wait=ev_xg[0])
            P.wait("ac", ev_eps)
            for t in range(NT + 1):
                step0(t)
            stats_cur = (MS, rstd0)
            for g in range(NPG):
                pe_guard = P.ev("pe")
                P.wait("pe", ev_bh, ev_pvc)
                if g == 0:
                    P.wait("pe", ev_rwt)
                    P.wait("dv", ev_rt0)
                sb_g, rstd_g = stats_cur
                png_read, ev_plng = ln_apply(xs, B3, sb_g, rstd_g,
                                             apply_wait=pe_guard)
                last_buf_apply[g % 2] = png_read[-1]
                P.wait("sy", *png_read)
                if g + 1 < NPG:
                    ev_nx = ev_xg[g + 1] = P.em("sy", lambda e, g=g: e.dma_start(
                        out=A[:, :].rearrange("p (t d) -> p t d", t=NT),
                        in_=xp3[:, g + 1]), inc="xpr", by=16)
                else:
                    ev_nx = ev_x_main = emit_x_load("xpr")
                P.wait("pe", ev_plng)
                ev_kh = proj(wkh_in, lambda eb: C[:, eb * 1024:(eb + 1) * 1024],
                             AF.Silu, n_eb=8)
                ev_vh = proj(wvh_in,
                             lambda eb: C[:, 8192 + eb * 1024:8192 + (eb + 1) * 1024],
                             AF.Silu, n_eb=8)
                sb_n = STT if g % 2 == 0 else MS
                stepn, rstdn = ln_stats_maker(
                    xs, sb_n, ac_wait=last_buf_apply[(g + 1) % 2],
                    dv_wait=ev_nx)
                P.wait("ac", ev_nx)
                stats_cur = (sb_n, rstdn)
                pkd_ev = {}
                pvc_ev = {}
                umm_ev = {}
                bh_ev = {}
                for c in range(NCH):
                    csl = slice(c * 128, (c + 1) * 128)
                    half = (c % 2) * 2048
                    kh0 = (c % 2) * 1024
                    P.wait("pe", ev_kh if c < 2 else None,
                           ev_vh if c < 2 else None,
                           ev_kdfpre if (g == 0 and c == 0) else None,
                           pkd_ev.get(c - 2), pvc_ev.get(c - 2))
                    for h in range(8):
                        P.em("pe", lambda e, h=h, csl=csl, half=half: e.transpose(
                            PCb[:, half + h * 128:half + (h + 1) * 128],
                            C[:, h * 1024:(h + 1) * 1024][:, csl], IDT[:, :]))
                    for h in range(8):
                        ev_ptr = P.em("pe", lambda e, h=h, csl=csl, half=half:
                                      e.transpose(
                                          PCb[:, half + 1024 + h * 128:half + 1024 + (h + 1) * 128],
                                          C[:, 8192 + h * 1024:8192 + (h + 1) * 1024][:, csl],
                                          IDT[:, :]), inc="pe" if h == 7 else None)
                    P.wait("dv", ev_ptr, umm_ev.get(c - 2))
                    ev_pkd = pkd_ev[c] = P.em("dv", lambda e, half=half, kh0=kh0:
                                              e.tensor_tensor(
                                                  KCH[:, kh0:kh0 + 1024],
                                                  PCb[:, half:half + 1024],
                                                  QD[:, 0:1024], ALU.mult),
                                              inc="dv")
                    P.wait("ac", ev_ptr, umm_ev.get(c - 2))
                    ev_pvc = pvc_ev[c] = P.em("ac", lambda e, half=half, kh0=kh0:
                                              e.copy(
                                                  VC[:, kh0:kh0 + 1024],
                                                  PCb[:, half + 1024:half + 2048]),
                                              inc="ac")
                    ps_u = PA if c % 2 == 0 else PB
                    P.wait("pe", pkd_ev[c], pvc_ev[c],
                           ev_vh if c < 2 else None, bh_ev.get(c - 2))
                    for h in range(8):
                        ev_umm = P.em("pe", lambda e, h=h, ps_u=ps_u, kh0=kh0:
                                      e.matmul(
                                          ps_u[:, h * 128:(h + 1) * 128],
                                          KCH[:, kh0 + h * 128:kh0 + (h + 1) * 128],
                                          VC[:, kh0 + h * 128:kh0 + (h + 1) * 128],
                                          start=True, stop=True),
                                      inc="pe" if h == 7 else None)
                    umm_ev[c] = ev_umm
                    P.wait("dv", ev_umm, ev_kpre, ev_cpre)
                    P.em("dv", lambda e: e.tensor_tensor(
                        SJb[:, 0:1024].rearrange("p (h n) -> p h n", h=8),
                        SJb[:, 0:1024].rearrange("p (h n) -> p h n", h=8),
                        MS[:, 112:120].rearrange("p (h o) -> p h o", o=1)
                        .broadcast_to([128, 8, 128]), ALU.mult), inc="dv")
                    P.em("dv", lambda e, ps_u=ps_u: e.tensor_tensor(
                        SJb[:, 0:1024], SJb[:, 0:1024], ps_u[:, 0:1024],
                        ALU.add), inc="dv")
                    ev_bh = bh_ev[c] = P.ev("dv")
                    stepn(c)
                stepn(NCH)
                proj_war[0] = bh_ev[NCH - 2]
                proj_war[1] = bh_ev[NCH - 1]
            def th_post_bh(e):
                r = load_rs(e)
                return e.dma_start(out=sh[bass.ts(r, 128), 0:1024],
                                  in_=SJb[:, 0:1024])
            P.wait("po", ev_bh)
            P.em("po", th_post_bh, inc="shb", by=16)
            P.wait("po", ("shb", 16))

            def th_post_f1(e):
                r = load_rs(e)
                return e.dma_start(out=shf[bass.ts(r, 1), 0:1], in_=FLG[0:1, 1:2])
            P.em("po", th_post_f1, inc="sf0", by=16)

            def th_poll1(e):
                with e.register("cnt") as cnt:
                    e.reg_mov(cnt, 0)

                    def cond():
                        return e.scalar_reg_alu(
                            ALU.not_equal,
                            e.value_load(FLG[0:1, 0:1].bitcast(mybir.dt.int32)),
                            int(np.float32(MAGIC).view(np.int32)))

                    rp_ = load_rp(e)
                    with e.While(cond):
                        e.reg_add(cnt, cnt, 16)
                        e.dma_start(out=FLG[0:1, 0:1],
                                    in_=shf[bass.ts(rp_, 1), 0:1]).then_inc(
                                        P.sems["pl1"], 16)
                        e.wait_ge(P.sems["pl1"], cnt)
                return e.nop()
            P.em("po", th_poll1, inc="po")
            P.em("po", lambda e: e.dma_start(out=SIb[:, 0:1024],
                                             in_=sh[0:128, 0:1024]),
                 inc="se1", by=16)
            P.em("po", lambda e: e.dma_start(out=SIb[:, 1024:2048],
                                             in_=sh[128:256, 0:1024]),
                 inc="se1", by=16)
            ev_e1 = ("se1", 32)
            ev_qd = build_qd(extra_wait=ev_pkd, redo_iota=True)
            ev_x = ("xpr", (NPG + 1) * 16)
        else:
            ev_x_main = None

        if prefix:
            pe_guard1 = P.ev("pe")
            sb1, rstd1 = stats_cur
            ln1_read, ev_ln1 = ln_apply(xs, B3, sb1, rstd1,
                                        apply_wait=pe_guard1)
        else:
            P.wait("dv", ev_x)
            P.wait("ac", ev_x, ev_eps)
            P.wait("pe", ev_rwt, ev_x)
            P.wait("dv", ev_rt0)
            ln1_read, ev_ln1 = emit_ln(xs, B3, extra_ac_wait=ev_nwb)

        for t in [4, 5, 6, 7, 0, 1, 2, 3]:
            P.wait("sy", ln1_read[t])
            P.em("sy", lambda e, t=t: e.dma_start(
                out=xspill[t * 128:(t + 1) * 128, :], in_=xs(t)), inc="isp", by=16)
        SPILLS = ("isp", 128)
        spill_ev = {t: SPILLS for t in range(NT)}

        P.wait("pe", ev_ln1, spill_ev[4], spill_ev[5], spill_ev[6], spill_ev[7])
        ev_k = proj(wk_in, lambda hh: kf(hh), AF.Silu)
        ev_v = proj(wv_in, lambda hh: vf(hh), AF.Silu)
        P.wait("pe", spill_ev[0], spill_ev[1], spill_ev[2], spill_ev[3])
        ev_q = proj(wq_in, lambda hh: qf(hh), AF.Silu)
        ev_g1 = proj(wg1_in, lambda _: G1[:, :], None, n_eb=1)

        if prefix:
            P.wait("dv", ev_e1, ev_coef)
            for h in range(H):
                P.em("dv", lambda e, h=h: e.tensor_scalar(
                    Ssb[:, h * 128:(h + 1) * 128],
                    SIb[:, h * 128:(h + 1) * 128],
                    COEF[:, h:h + 1], None, ALU.mult), inc="dv")
            ev_sinit = P.ev("dv")
            ev_scp = ev_sinit
        else:
            ev_si0 = P.em("dv", lambda e: e.memset(SIb, 0.0), inc="dv")
            prev_sij = None
            for j in range(NCORES):
                P.wait("sy", prev_sij)
                ev_sj = P.em("sy", lambda e, j=j: e.dma_start(
                    out=SJb, in_=s_all[j * 128:(j + 1) * 128, :]), inc="isa", by=16)
                P.wait("dv", ev_sj, ev_coef)
                for h in range(H):
                    P.em("dv", lambda e, j=j, h=h: e.scalar_tensor_tensor(
                        SIb[:, h * 128:(h + 1) * 128],
                        SJb[:, h * 128:(h + 1) * 128],
                        COEF[:, j * 16 + h:j * 16 + h + 1],
                        SIb[:, h * 128:(h + 1) * 128], ALU.mult, ALU.add),
                        inc="dv")
                prev_sij = P.ev("dv")
            ev_sinit = P.ev("dv")

        if not prefix:
            ev_scp = P.em("dv", lambda e: e.tensor_copy(Ssb, SIb), inc="dv")
        ev_oev = None
        ev_supd = None
        ev_umm_prev = None
        for c in range(NCH):
            csl = slice(c * 128, (c + 1) * 128)
            P.wait("pe", ev_oev, ev_k if c == 0 else None, ev_v if c == 0 else None,
                   ev_q if c == 0 else None, ev_g1 if c == 0 else None)
            for h in range(H):
                P.em("pe", lambda e, h=h, csl=csl: e.transpose(
                    PCb[:, h * 128:(h + 1) * 128], kf(h, csl), IDT[:, :]))
            for h in range(H):
                ev_tr = P.em("pe", lambda e, h=h, csl=csl: e.transpose(
                    PCb[:, 2048 + h * 128:2048 + (h + 1) * 128], vf(h, csl),
                    IDT[:, :]), inc="pe" if h == H - 1 else None)
            P.wait("dv", ev_tr, ev_umm_prev, ev_scp if c == 0 else None,
                   ev_sinit if c == 0 else None, ev_q if c == 0 else None,
                   ev_kdf if c == 0 else None)
            ev_kp = P.em("dv", lambda e: e.tensor_tensor(
                KCH[:, :], PCb[:, 0:2048], KDF[:, :], ALU.mult), inc="dv")
            ev_qp = P.em("dv", lambda e, c=c: e.tensor_tensor(
                LS[:, :].rearrange("p (h n) -> p h n", h=H),
                Ab[:, 0:16384].rearrange("p (h n) -> p h n", h=H)[:, :, c * 128:(c + 1) * 128],
                QD[:, :].rearrange("p (h n) -> p h n", h=H), ALU.mult), inc="dv")
            P.wait("ac", ev_tr, ev_umm_prev, ev_supd,
                   ev_scp if c == 0 else None, ev_sinit if c == 0 else None)
            ev_vch = P.em("ac", lambda e: e.copy(VC[:, :], PCb[:, 2048:4096]),
                          inc="ac")
            ev_sbf = P.em("ac", lambda e: e.copy(SB16[:, :], Ssb), inc="ac")
            P.wait("pe", ev_qp, ev_sbf)
            for h in range(H):
                ps = PA if h < 8 else PB
                off = (h % 8) * 128
                ev_sc = P.em("pe", lambda e, h=h, ps=ps, off=off, csl=csl: e.matmul(
                    ps[:, off:off + 128], kf(h, csl), qf(h, csl),
                    start=True, stop=True), inc="pe" if h == H - 1 else None)
            P.wait("dv", ev_sc, ("shb", 16) if (prefix and c == 0) else None)
            P.em("dv", lambda e: e.tensor_tensor(
                SCB[:, 0:1024], PA[:, :], DT[:, 0:1024], ALU.mult), inc="dv")
            ev_scb = P.em("dv", lambda e: e.tensor_tensor(
                SCB[:, 1024:2048], PB[:, :], DT[:, 1024:2048], ALU.mult), inc="dv")
            P.wait("pe", ev_scb)
            for h in range(H):
                hsl = slice(h * 128, (h + 1) * 128)
                P.em("pe", lambda e, hsl=hsl: e.matmul(
                    PC[:, hsl], SCB[:, hsl], VC[:, hsl], start=True, stop=False))
                ev_omm = P.em("pe", lambda e, hsl=hsl: e.matmul(
                    PC[:, hsl], LS[:, hsl], SB16[:, hsl], start=False, stop=True),
                    inc="pe" if h == H - 1 else None)
            for h in range(H):
                ps = PA if h < 8 else PB
                off = (h % 8) * 128
                hsl = slice(h * 128, (h + 1) * 128)
                ev_umm = P.em("pe", lambda e, ps=ps, off=off, hsl=hsl: e.matmul(
                    ps[:, off:off + 128], KCH[:, hsl], VC[:, hsl],
                    start=True, stop=True), inc="pe" if h == H - 1 else None)
            ev_umm_prev = ev_umm
            P.wait("dv", ev_umm)
            P.em("dv", lambda e: e.tensor_tensor(
                Ssb[:, :].rearrange("p (h n) -> p h n", h=H),
                Ssb[:, :].rearrange("p (h n) -> p h n", h=H),
                CDB[:, :].rearrange("p (h o) -> p h o", o=1)
                .broadcast_to([128, H, 128]), ALU.mult), inc="dv")
            P.em("dv", lambda e: e.tensor_tensor(
                Ssb[:, 0:1024], Ssb[:, 0:1024], PA[:, :], ALU.add), inc="dv")
            P.em("dv", lambda e: e.tensor_tensor(
                Ssb[:, 1024:2048], Ssb[:, 1024:2048], PB[:, :], ALU.add),
                inc="dv")
            ev_supd = P.ev("dv")
            P.wait("ac", ev_omm)
            ev_oev = P.em("ac", lambda e, c=c: e.copy(ob(c), PC[:, :]), inc="ac")

        ev_scan = ev_oev
        P.wait("sy", ev_supd)
        ev_sloc = P.em("sy", lambda e: e.dma_start(out=sloc_out, in_=Ssb),
                       inc="slc", by=16)

        if prefix:
            def th_post_s(e):
                r = e.value_load(SLT[0:1, 0:1])
                return e.dma_start(out=sh[bass.ts(r, 128), 1024:3072], in_=Ssb)
            P.wait("po", ev_supd)
            P.em("po", th_post_s, inc="shs", by=16)
            P.wait("po", ("shs", 16))

            def th_post_f2(e):
                r = e.value_load(SLT[0:1, 0:1])
                return e.dma_start(out=shf[bass.ts(r, 1), 1:2], in_=FLG[0:1, 1:2])
            P.em("po", th_post_f2, inc="sf1", by=16)
            P.em("po", lambda e: e.memset(FLG[0:1, 0:1], 0.0), inc="po")

            def th_poll2(e):
                with e.register("cnt2") as cnt:
                    e.reg_mov(cnt, 0)

                    def cond():
                        return e.scalar_reg_alu(
                            ALU.not_equal,
                            e.value_load(FLG[0:1, 0:1].bitcast(mybir.dt.int32)),
                            int(np.float32(MAGIC).view(np.int32)))

                    rp_ = e.value_load(SLT[0:1, 1:2])
                    with e.While(cond):
                        e.reg_add(cnt, cnt, 16)
                        e.dma_start(out=FLG[0:1, 0:1],
                                    in_=shf[bass.ts(rp_, 1), 1:2]).then_inc(
                                        P.sems["pl2"], 16)
                        e.wait_ge(P.sems["pl2"], cnt)
                return e.nop()
            P.em("po", th_poll2, inc="po")

            def th_read_s(e):
                r = e.value_load(SLT[0:1, 1:2])
                return e.dma_start(out=SJb, in_=sh[bass.ts(r, 128), 1024:3072])
            P.em("po", th_read_s, inc="se2", by=16)

            P.wait("dv", ("se2", 16), ev_c1n)
            for h in range(H):
                P.em("dv", lambda e, h=h: e.tensor_scalar(
                    SJb[:, h * 128:(h + 1) * 128], SJb[:, h * 128:(h + 1) * 128],
                    COEF[:, 16 + h:17 + h], None, ALU.mult), inc="dv")
            for h in range(H):
                P.em("dv", lambda e, h=h: e.scalar_tensor_tensor(
                    SJb[:, h * 128:(h + 1) * 128], SIb[:, h * 128:(h + 1) * 128],
                    COEF[:, 48 + h:49 + h], SJb[:, h * 128:(h + 1) * 128],
                    ALU.mult, ALU.add), inc="dv")
            ev_dS = P.ev("dv")

        ev_dmp = []
        if "d_ob" in dump_t:
            P.wait("po", ev_scan)
            ev_dmp.append(P.em("po", lambda e: e.dma_start(
                out=dump_t["d_ob"], in_=B[:, :]), inc="dst", by=16))
        if "d_q" in dump_t:
            P.wait("po", ev_q)
            ev_dmp.append(P.em("po", lambda e: e.dma_start(
                out=dump_t["d_q"], in_=Ab[:, 0:16384]), inc="dst", by=16))
        if "d_k" in dump_t:
            P.wait("po", ev_k)
            ev_dmp.append(P.em("po", lambda e: e.dma_start(
                out=dump_t["d_k"], in_=Ab[:, 16384:32768]), inc="dst", by=16))
        if "d_v" in dump_t:
            P.wait("po", ev_v)
            ev_dmp.append(P.em("po", lambda e: e.dma_start(
                out=dump_t["d_v"], in_=C[:, :]), inc="dst", by=16))

        def emit_reload():
            return P.em("sy", lambda e: e.dma_start(
                out=A[:, :].rearrange("p (t d) -> p t d", t=NT),
                in_=xspill.rearrange("(t p) d -> p t d", p=128)), inc="irl", by=16)

        P.wait("sy", ev_scan, *ev_dmp)
        if not prefix:
            ev_rel = emit_reload()

        P.wait("pe", ev_supd, ev_scan, ev_wg2, ev_g1)
        if prefix:
            P.wait("ac", ("shs", 16))
        P.wait("dv", *ev_dmp)
        P.wait("ac", ("slc", 16))
        sig_ev = {}
        obg_ev = {}
        cadd_ev = {}
        cmm_ev = {}
        ds16_ev = {}
        qp2_ev = {}
        FAC = COEF[:, 64:80]
        for t in range(NT + 2):
            if prefix:
                if t < NT:
                    qbuf = KCH if t % 2 == 0 else KDF
                    dbuf = SB16 if t % 2 == 0 else DT
                    P.wait("po", cmm_ev.get(t - 2),
                           ev_supd if t <= 1 else None)
                    qp2_ev[t] = P.em("po", lambda e, t=t, qbuf=qbuf:
                                     e.tensor_tensor(
                                         qbuf[:, :].rearrange("p (h n) -> p h n", h=H),
                                         Ab[:, 0:16384].rearrange("p (h n) -> p h n", h=H)[:, :, t * 128:(t + 1) * 128],
                                         QD[:, :].rearrange("p (h n) -> p h n", h=H),
                                         ALU.mult), inc="po")
                    P.wait("dv", cmm_ev.get(t - 2))
                    if t == 0:
                        ds16_ev[0] = P.em("dv", lambda e: e.tensor_copy(
                            SB16[:, :], SJb), inc="dv")
                        P.em("dv", lambda e: e.tensor_copy(
                            FAC[:, :], CDB[:, :]), inc="dv")
                    else:
                        ds16_ev[t] = P.em("dv", lambda e, dbuf=dbuf:
                                          e.tensor_tensor(
                                              dbuf[:, :].rearrange("p (h n) -> p h n", h=H),
                                              SJb[:, :].rearrange("p (h n) -> p h n", h=H),
                                              FAC[:, :].rearrange("p (h o) -> p h o", o=1)
                                              .broadcast_to([128, H, 128]),
                                              ALU.mult), inc="dv")
                        if t + 1 < NT:
                            P.em("dv", lambda e: e.tensor_tensor(
                                FAC[:, :], FAC[:, :], CDB[:, :], ALU.mult),
                                inc="dv")
                tc = t - 1
                if 0 <= tc < NT:
                    P.wait("dv", cmm_ev[tc], ev_scan)
                    P.em("dv", lambda e, tc=tc: e.tensor_tensor(
                        ob(tc)[:, 0:1024], PA[:, :], ob(tc)[:, 0:1024],
                        ALU.add), inc="dv")
                    cadd_ev[tc] = P.em("dv", lambda e, tc=tc: e.tensor_tensor(
                        ob(tc)[:, 1024:2048], PB[:, :], ob(tc)[:, 1024:2048],
                        ALU.add), inc="dv")
                if t < NT:
                    qbuf = KCH if t % 2 == 0 else KDF
                    dbuf = SB16 if t % 2 == 0 else DT
                    P.wait("pe", ds16_ev[t], qp2_ev[t], cadd_ev.get(t - 1))
                    for h in range(H):
                        ps = PA if h < 8 else PB
                        off = (h % 8) * 128
                        hsl = slice(h * 128, (h + 1) * 128)
                        cmm_ev[t] = P.em("pe", lambda e, ps=ps, off=off,
                                         hsl=hsl, qbuf=qbuf, dbuf=dbuf:
                                         e.matmul(
                                             ps[:, off:off + 128], qbuf[:, hsl],
                                             dbuf[:, hsl], start=True,
                                             stop=True),
                                         inc="pe" if h == H - 1 else None)
            if t < NT:
                P.wait("pe", sig_ev.get(t - 1))
                for c4 in range(4):
                    ev_g = P.em("pe", lambda e, t=t, c4=c4: e.matmul(
                        PC[:, c4 * 512:(c4 + 1) * 512],
                        G1[:, t * 128:(t + 1) * 128],
                        WG2[:, c4 * 512:(c4 + 1) * 512],
                        start=True, stop=True),
                        inc="pe" if c4 == 3 else None)
                sbuf_g = LS if t % 2 == 0 else VC
                P.wait("ac", ev_g, ev_scan)
                sig_ev[t] = P.em("ac", lambda e, sbuf_g=sbuf_g: e.activation(
                    sbuf_g[:, :], PC[:, :], AF.Sigmoid), inc="ac")
            tp_ = t - 1
            if 0 <= tp_ < NT:
                sbuf_gp = LS if tp_ % 2 == 0 else VC
                P.wait("dv", sig_ev[tp_])
                obg_ev[tp_] = P.em("dv", lambda e, tp_=tp_, sbuf_gp=sbuf_gp:
                                   e.tensor_tensor(ob(tp_), ob(tp_),
                                                   sbuf_gp[:, :], ALU.mult),
                                   inc="dv")
                P.wait("ac", obg_ev[tp_])
                ev_sq = P.em("ac", lambda e, tp_=tp_: e.activation(
                    TMPF[:, :], ob(tp_), AF.Square), inc="ac")
                P.wait("dv", ev_sq)
                P.em("dv", lambda e, tp_=tp_: e.tensor_reduce(
                    STT[:, tp_ * 16:(tp_ + 1) * 16].rearrange(
                        "p (h o) -> p h o", h=H),
                    TMPF[:, :].rearrange("p (h n) -> p h n", h=H), AX, ALU.add),
                    inc="dv")
        ev_rms = P.ev("dv")
        if prefix:
            P.wait("sy", cadd_ev[NT - 1])
            ev_rel = emit_reload()
        P.wait("ac", ev_rms)
        P.em("ac", lambda e: e.activation(RSTD[:, :], STT[:, :], AF.Ln,
                                          scale=1.0 / HD, bias=EPSC[:, 0:1]), inc="ac")
        ev_rstd = P.em("ac", lambda e: e.activation(
            RSTD[:, :], RSTD[:, :], AF.Exp, scale=-0.5), inc="ac")
        P.wait("dv", ev_rstd, ev_nw2)
        on_ev = {}
        tr_ev = {}
        cfm_ev = {}
        wob_ev = {}
        RTb16 = RT[:, 0:64].bitcast(BF16)
        ev_rb16 = P.em("dv", lambda e: e.tensor_copy(RTb16, RSTD[:, :]),
                       inc="dv")
        for t in range(NT + 1):
            if t < NT:
                buf = LS if t % 2 == 0 else VC
                P.wait("dv", tr_ev.get(t - 2), sig_ev.get(NT - 1) if t <= 1 else None)
                P.em("dv", lambda e, t=t, buf=buf: e.tensor_tensor(
                    buf[:, :], ob(t), NW[:, :], ALU.mult), inc="dv")
                P.em("dv", lambda e, t=t, buf=buf: e.tensor_tensor(
                    buf[:, :].rearrange("p (h n) -> p h n", h=H),
                    buf[:, :].rearrange("p (h n) -> p h n", h=H),
                    RTb16[:, t * 16:(t + 1) * 16]
                    .rearrange("p (h o) -> p h o", o=1)
                    .broadcast_to([128, H, 128]), ALU.mult), inc="dv")
                on_ev[t] = P.ev("dv")
                P.wait("po", on_ev[t], *ev_dmp)
                wob_ev[t] = P.em("po", lambda e, t=t: e.dma_start(
                    out=B[:, t * 2048:(t + 1) * 2048].rearrange(
                        "p (g d) -> p g d", g=1),
                    in_=wo_in[t * 128:(t + 1) * 128, :].rearrange(
                        "(g p) d -> p g d", p=128)), inc="swb", by=16)
            tp_ = t - 1
            if 0 <= tp_ < NT:
                buf = LS if tp_ % 2 == 0 else VC
                half = (tp_ % 2) * 2048
                P.wait("pe", on_ev[tp_], cfm_ev.get(tp_ - 2))
                for k in range(KC):
                    tr_ev[tp_] = P.em("pe", lambda e, k=k, buf=buf, half=half:
                                      e.transpose(
                                          PCb[:, half + k * 128:half + (k + 1) * 128],
                                          buf[:, k * 128:(k + 1) * 128],
                                          IDT[:, :]), inc="pe" if k == KC - 1 else None)
                P.wait("ac", tr_ev[tp_])
                cfm_ev[tp_] = P.em("ac", lambda e, tp_=tp_, half=half: e.copy(
                    C3[:, :, tp_ * 128:(tp_ + 1) * 128],
                    PCb[:, half:half + 2048].rearrange("p (k n) -> p k n", k=KC)),
                    inc="ac")
        ev_onorm = cfm_ev[NT - 1]

        ev_wob = ("swb", 8 * 16)

        wo_war = {0: None, 1: None}

        def wo_group(ks, ev_ready):
            for t in range(NT):
                even = (t % 2 == 0)
                pss = [(PA, 0), (PA, 512), (PB, 0), (PB, 512)] if even else \
                      [(PC, 0), (PC, 512), (PC, 1024), (PC, 1536)]
                P.wait("pe", ev_ready, wo_war[t % 2])
                nk = len(ks)
                for ki, (k, rhs_fn) in enumerate(ks):
                    for c4, (ps, off) in enumerate(pss):
                        ev_mm = P.em("pe", lambda e, k=k, t=t, ps=ps, off=off,
                                     c4=c4, rhs_fn=rhs_fn, ki=ki: e.matmul(
                                         ps[:, off:off + 512],
                                         C3[:, k, t * 128:(t + 1) * 128],
                                         rhs_fn(c4), start=(ki == 0),
                                         stop=(ki == nk - 1)),
                                     inc="pe" if (ki == nk - 1 and c4 == 3) else None)
                P.wait("dv", ev_mm)
                for c4, (ps, off) in enumerate(pss):
                    P.em("dv", lambda e, t=t, ps=ps, off=off, c4=c4: e.tensor_tensor(
                        xs(t)[:, c4 * 512:(c4 + 1) * 512],
                        ps[:, off:off + 512],
                        xs(t)[:, c4 * 512:(c4 + 1) * 512], ALU.add), inc="dv")
                wo_war[t % 2] = P.ev("dv")

        P.wait("dv", ev_rel)
        P.wait("pe", sig_ev[NT - 1], ev_rms, ev_onorm)
        B8 = B[:, :].rearrange("p (g d) -> p g d", g=8)
        grp0 = [(k, (lambda k_: (lambda c4: B8[:, k_, c4 * 512:(c4 + 1) * 512]))(k))
                for k in range(8)]
        wo_group(grp0, ev_wob)
        for grp_ks in ((8, 9, 10), (11, 12, 13), (14, 15)):
            ks = []
            for k in grp_ks:
                s, ev_l = load_wslot(
                    wo_in[k * 128:(k + 1) * 128, :].rearrange(
                        "(g p) d -> p g d", p=128))
                P.wait("pe", ev_l)
                ks.append((k, (lambda s_: (lambda c4: wslot[s_][:, c4 * 512:(c4 + 1) * 512]))(s)))
                ring_owner[s] = None
            wo_group(ks, None)
            last_pe = P.ev("pe")
            for s in range(3):
                if ring_owner[s] is None:
                    ring_owner[s] = last_pe
        ev_x1 = P.ev("dv")

        if "d_x1" in dump_t:
            P.wait("sy", ev_x1)
            ev_dx1 = P.em("sy", lambda e: e.dma_start(
                out=dump_t["d_x1"], in_=A[:, :]), inc="dxs", by=16)
            P.wait("dv", ev_dx1)

        n_y_stores = 0
        if upto == "attn":
            P.wait("sy", ev_x1)
            for t in range(NT):
                P.em("sy", lambda e, t=t: e.dma_start(
                    out=y_out[t * 128:(t + 1) * 128, :], in_=xs(t)),
                    inc="st", by=16)
                n_y_stores += 1
            P.wait("sy", P.ev("st"), P.ev("dst"), P.ev("slc"), P.ev("dxs"))
        else:
            P.wait("ac", ev_x1)
            P.wait("dv", ev_x1)
            ln2_read, ev_ln2 = emit_ln(xs, B3)

            P.wait("pe", ev_ln2, ev_rwt)
            for k in range(KC):
                P.em("pe", lambda e, k=k: e.matmul(
                    PA[0:8, 0:512], RWT[:, k * 8:(k + 1) * 8],
                    fmb(k, slice(0, 512)), start=(k == 0), stop=(k == KC - 1)))
                ev_rl = P.em("pe", lambda e, k=k: e.matmul(
                    PA[0:8, 512:1024], RWT[:, k * 8:(k + 1) * 8],
                    fmb(k, slice(512, 1024)), start=(k == 0), stop=(k == KC - 1)),
                    inc="pe" if k == KC - 1 else None)
            P.wait("dv", ev_rl)
            ev_rcp = P.em("dv", lambda e: e.tensor_copy(LS[0:8, 0:1024], PA[0:8, :]),
                          inc="dv")
            P.wait("pe", ev_rcp)
            for tt in range(NT):
                ev_rtr = P.em("pe", lambda e, tt=tt: e.transpose(
                    PCb[:, tt * 8:(tt + 1) * 8],
                    LS[0:8, tt * 128:(tt + 1) * 128], IDT[0:8, 0:8]),
                    inc="pe" if tt == NT - 1 else None)
            P.wait("dv", ev_rtr)
            ev_rlg = P.em("dv", lambda e: e.tensor_copy(RT[:, 0:64], PCb[:, 0:64]),
                          inc="dv")

            for tt in range(NT):
                lg = RT[:, tt * 8:(tt + 1) * 8]
                ex = RT[:, 64 + tt * 8:64 + (tt + 1) * 8]
                g2 = RT[:, 168 + tt * 8:168 + (tt + 1) * 8]
                nm = RT[:, 128 + tt:128 + tt + 1]
                sm_ = RT[:, 136 + tt:136 + tt + 1]
                m1 = RT[:, 144 + tt:144 + tt + 1]
                m2 = RT[:, 152 + tt:152 + tt + 1]
                rd = RT[:, 160 + tt:160 + tt + 1]
                P.em("dv", lambda e, lg=lg, nm=nm: e.reduce_max(
                    nm, lg, AX, negate=True), inc="dv")
                ev1 = P.ev("dv")
                P.wait("ac", ev1)
                ev2 = P.em("ac", lambda e, lg=lg, ex=ex, nm=nm: e.activation(
                    ex, lg, AF.Exp, bias=nm), inc="ac")
                P.wait("dv", ev2)
                P.em("dv", lambda e, ex=ex, sm_=sm_: e.reduce_sum(sm_, ex, AX),
                     inc="dv")
                P.em("dv", lambda e, sm_=sm_: e.reciprocal(sm_, sm_), inc="dv")
                P.em("dv", lambda e, ex=ex, sm_=sm_: e.tensor_scalar(
                    ex, ex, sm_, None, ALU.mult), inc="dv")
                P.em("dv", lambda e, ex=ex, m1=m1: e.reduce_max(m1, ex, AX),
                     inc="dv")
                P.em("dv", lambda e, ex=ex, m1=m1, g2=g2: e.tensor_scalar(
                    g2, ex, m1, None, ALU.is_ge), inc="dv")
                P.em("dv", lambda e, ex=ex, g2=g2: e.scalar_tensor_tensor(
                    g2, g2, -1e30, ex, ALU.mult, ALU.add), inc="dv")
                P.em("dv", lambda e, g2=g2, m2=m2: e.reduce_max(m2, g2, AX),
                     inc="dv")
                P.em("dv", lambda e, m1=m1, m2=m2, rd=rd: e.tensor_tensor(
                    rd, m1, m2, ALU.add), inc="dv")
                P.em("dv", lambda e, rd=rd: e.tensor_scalar(
                    rd, rd, 1e-20, None, ALU.add), inc="dv")
                P.em("dv", lambda e, rd=rd: e.reciprocal(rd, rd), inc="dv")
                P.em("dv", lambda e, ex=ex, m2=m2, g2=g2: e.tensor_scalar(
                    g2, ex, m2, None, ALU.is_ge), inc="dv")
                P.em("dv", lambda e, ex=ex, g2=g2: e.tensor_tensor(
                    ex, ex, g2, ALU.mult), inc="dv")
                P.em("dv", lambda e, ex=ex, rd=rd, tt=tt: e.tensor_scalar(
                    G[:, tt * 8:(tt + 1) * 8], ex, rd, None, ALU.mult), inc="dv")
            ev_G = P.ev("dv")

            if "d_G" in dump_t:
                P.wait("sy", ev_G)
                P.em("sy", lambda e: e.dma_start(
                    out=dump_t["d_G"][:, 0:64], in_=G[:, :]), inc="dxs", by=16)

            wd_buf = [SRb[:, 0:5632], SRb[:, 5632:11264]]
            wd_owner = [ev_ln2, ev_ln2]
            wd_ctr = [0]

            def load_wd_quarter(wd_ap, dq):
                b = wd_ctr[0] % 2
                wd_ctr[0] += 1
                P.wait("po", wd_owner[b])
                src = wd_ap.rearrange("(g p) d -> p g d", p=128)[:, :, dq * 512:(dq + 1) * 512]
                ev = P.em("po", lambda e, b=b, src=src: e.dma_start(
                    out=wd_buf[b][:, :].rearrange("p (g d) -> p g d", g=NJ),
                    in_=src), inc=f"wd{b}", by=16)
                return b, ev

            experts = []
            for e_i in range(8):
                experts.append((wgE[e_i], wuE[e_i], wdE[e_i], e_i))
            experts.append((sg_in[:, 0:HE], su_in[:, 0:HE], sd_in[0:HE, :], None))
            experts.append((sg_in[:, HE:2 * HE], su_in[:, HE:2 * HE],
                            sd_in[HE:2 * HE, :], None))

            ev_ph2_mm_prev = P.ev("pe")
            ev_evac_prev = ev_G
            ring_n[0] = 5
            ring_owner[3] = ev_ph2_mm_prev
            ring_owner[4] = ev_ph2_mm_prev
            evac_evs = {}
            wd_pending = []
            for (ei, (wg_ap, wu_ap, wd_ap, gsel)) in enumerate(experts):
                wdq0 = load_wd_quarter(wd_ap, 0)
                wd_pending = [wdq0]
                ev_h = {}
                for j in range(NJ):
                    sg_, ev_lg = load_wslot(eblk(wg_ap, j * 128))
                    su_, ev_lu = load_wslot(eblk(wu_ap, j * 128))
                    if j % 2 == 0:
                        psg, psu = PA, PB
                        psg_sl = (0, 1024)
                    else:
                        psg, psu = PC, PC
                    P.wait("pe", ev_lg, ev_lu,
                           ev_h.get(j - 2), ev_evac_prev if j <= 1 else None)
                    for k in range(KC):
                        if j % 2 == 0:
                            g_lo, g_hi = psg[:, 0:512], psg[:, 512:1024]
                            u_lo, u_hi = psu[:, 0:512], psu[:, 512:1024]
                        else:
                            g_lo, g_hi = PC[:, 0:512], PC[:, 512:1024]
                            u_lo, u_hi = PC[:, 1024:1536], PC[:, 1536:2048]
                        P.em("pe", lambda e, sg_=sg_, k=k, g_lo=g_lo: e.matmul(
                            g_lo, wslot[sg_][:, k * 128:(k + 1) * 128],
                            fmb(k, slice(0, 512)),
                            start=(k == 0), stop=(k == KC - 1)))
                        P.em("pe", lambda e, sg_=sg_, k=k, g_hi=g_hi: e.matmul(
                            g_hi, wslot[sg_][:, k * 128:(k + 1) * 128],
                            fmb(k, slice(512, 1024)),
                            start=(k == 0), stop=(k == KC - 1)))
                        P.em("pe", lambda e, su_=su_, k=k, u_lo=u_lo: e.matmul(
                            u_lo, wslot[su_][:, k * 128:(k + 1) * 128],
                            fmb(k, slice(0, 512)),
                            start=(k == 0), stop=(k == KC - 1)))
                        ev_mm = P.em("pe", lambda e, su_=su_, k=k, u_hi=u_hi:
                                     e.matmul(
                                         u_hi, wslot[su_][:, k * 128:(k + 1) * 128],
                                         fmb(k, slice(512, 1024)),
                                         start=(k == 0), stop=(k == KC - 1)),
                                     inc="pe" if k == KC - 1 else None)
                    ring_owner[sg_] = ev_mm
                    ring_owner[su_] = ev_mm
                    P.wait("ac", ev_mm, ev_h.get(j - 1))
                    if j % 2 == 0:
                        hg_ap, hu_ap = PA[:, :], PB[:, :]
                    else:
                        hg_ap, hu_ap = PC[:, 0:1024], PC[:, 1024:2048]
                    ev_si = P.em("ac", lambda e, hg_ap=hg_ap: e.activation(
                        LS[:, 0:1024], hg_ap, AF.Silu), inc="ac")
                    P.wait("dv", ev_si, ev_ph2_mm_prev if j == 0 else None)
                    ev_h[j] = P.em("dv", lambda e, j=j, hu_ap=hu_ap: e.tensor_tensor(
                        C[:, j * 1024:(j + 1) * 1024], LS[:, 0:1024], hu_ap,
                        ALU.mult), inc="dv")
                ev_ph1 = ev_h[NJ - 1]
                ev_ph2_mm = None
                for dq in range(4):
                    bsel, ev_wd = wd_pending[-1]
                    if dq < 3:
                        wd_pending.append(load_wd_quarter(wd_ap, dq + 1))
                    for t in range(NT):
                        qsl = t % 4
                        ps, off = [(PA, 0), (PA, 512), (PB, 0), (PB, 512)][qsl]
                        P.wait("pe", ev_wd, ev_ph1 if (dq == 0 and t == 0) else None,
                               evac_evs.get((dq, t - 4) if t >= 4 else (dq - 1, t + 4)))
                        for j in range(NJ):
                            ev_mm2 = P.em("pe", lambda e, j=j, t=t, ps=ps, off=off,
                                          bsel=bsel: e.matmul(
                                              ps[:, off:off + 512],
                                              C[:, j * 1024 + t * 128:j * 1024 + (t + 1) * 128],
                                              wd_buf[bsel][:, :].rearrange(
                                                  "p (g d) -> p g d", g=NJ)[:, j, :],
                                              start=(j == 0), stop=(j == NJ - 1)),
                                          inc="pe" if j == NJ - 1 else None)
                        ev_ph2_mm = ev_mm2
                        P.wait("dv", ev_mm2)
                        if gsel is None:
                            evv = P.em("dv", lambda e, t=t, dq=dq, ps=ps, off=off:
                                       e.tensor_tensor(
                                           xs(t)[:, dq * 512:(dq + 1) * 512],
                                           ps[:, off:off + 512],
                                           xs(t)[:, dq * 512:(dq + 1) * 512],
                                           ALU.add), inc="dv")
                        else:
                            evv = P.em("dv", lambda e, t=t, dq=dq, ps=ps, off=off,
                                       gsel=gsel: e.scalar_tensor_tensor(
                                           xs(t)[:, dq * 512:(dq + 1) * 512],
                                           ps[:, off:off + 512],
                                           G[:, t * 8 + gsel:t * 8 + gsel + 1],
                                           xs(t)[:, dq * 512:(dq + 1) * 512],
                                           ALU.mult, ALU.add), inc="dv")
                        evac_evs[(dq, t)] = evv
                        if ei == len(experts) - 1 and dq == 3:
                            P.wait("sy", evv)
                            P.em("sy", lambda e, t=t: e.dma_start(
                                out=y_out[t * 128:(t + 1) * 128, :], in_=xs(t)),
                                inc="st", by=16)
                            n_y_stores += 1
                    wd_owner[bsel] = ev_ph2_mm
                ev_ph2_mm_prev = ev_ph2_mm
                ev_evac_prev = evac_evs[(3, NT - 1)]

            P.wait("sy", P.ev("st"), P.ev("dst"), P.ev("slc"), P.ev("dxs"))

        with nc.Block() as block:
            @block.sync
            def _(eng):
                for th in P.thunks["sy"]:
                    th(eng)

            @block.gpsimd
            def _(eng):
                for th in P.thunks["po"]:
                    th(eng)

            @block.tensor
            def _(eng):
                for th in P.thunks["pe"]:
                    th(eng)

            @block.vector
            def _(eng):
                for th in P.thunks["dv"]:
                    th(eng)

            @block.scalar
            def _(eng):
                for th in P.thunks["ac"]:
                    th(eng)

    return nc


_CACHE = {}


def _get_nc(upto="full", dumps=(), prefix=False):
    key = (upto, tuple(sorted(dumps)), prefix)
    if key not in _CACHE:
        _CACHE[key] = build(upto, dumps, prefix)
    return _CACHE[key]


def _mk_inmaps(inputs, s_all_list):
    x = np.ascontiguousarray(inputs["L0_attn_x"].reshape(8192, D))
    ld = np.ascontiguousarray(inputs["L0_attn_log_decay"].reshape(1, H))
    nw = np.ascontiguousarray(inputs["L0_attn_norm_w"].reshape(1, D))
    shared = {
        "ldec": ld, "nw": nw,
        "rw": inputs["L0_ffn_router_w"],
        "wq": inputs["L0_attn_Wq"], "wk": inputs["L0_attn_Wk"],
        "wv": inputs["L0_attn_Wv"], "wo": inputs["L0_attn_Wo"],
        "wg1": inputs["L0_attn_Wg1"], "wg2": inputs["L0_attn_Wg2"],
        "wgE": inputs["L0_ffn_wg"], "wuE": inputs["L0_ffn_wu"],
        "wdE": inputs["L0_ffn_wd"],
        "sg": inputs["L0_ffn_sg"], "su": inputs["L0_ffn_su"],
        "sd": inputs["L0_ffn_sd"],
    }
    shared = {k: np.ascontiguousarray(v) for k, v in shared.items()}
    maps = []
    for i in range(NCORES):
        pce = np.full((1, NCORES), 1e9, dtype=np.float32)
        for j in range(i):
            pce[0, j] = 1024.0 * (i - 1 - j)
        m = dict(shared)
        m["x"] = np.ascontiguousarray(x[i * T:(i + 1) * T])
        m["pce"] = pce
        m["s_all"] = s_all_list[i]
        maps.append(m)
    return maps


def _mk_inmaps_prefix(inputs, loopback=False):
    zero_s = np.zeros((NCORES * HD, D), dtype=np.float32)
    maps = _mk_inmaps(inputs, [zero_s] * NCORES)
    x = np.ascontiguousarray(inputs["L0_attn_x"].reshape(8192, D))
    ld = inputs["L0_attn_log_decay"].reshape(H)
    wk = inputs["L0_attn_Wk"]
    wv = inputs["L0_attn_Wv"]
    for i in range(NCORES):
        p = i // 2
        half = i % 2
        npre_real = p * 2048
        xpre = np.zeros((NPRE, D), dtype=np.float32)
        if npre_real:
            xpre[NPRE - npre_real:] = x[0:npre_real]
        m = maps[i]
        m["xpre"] = xpre
        m["wkh"] = np.ascontiguousarray(wk[:, half * 1024:(half + 1) * 1024])
        m["wvh"] = np.ascontiguousarray(wv[:, half * 1024:(half + 1) * 1024])
        m["ldh"] = np.ascontiguousarray(ld[half * 8:(half + 1) * 8].reshape(1, 8))
        if half == 0:
            m["mg"] = np.array([[0.0, 1e9, 1e9]], dtype=np.float32)
        else:
            m["mg"] = np.array([[1024.0, 0.0, 1024.0]], dtype=np.float32)
        m["slot"] = np.array([[half]], dtype=np.int32)
        m["pslot"] = np.array([[half if loopback else 1 - half]], dtype=np.int32)
    return maps


def kernel(**inputs):
    nc = _get_nc("full", prefix=True)
    maps = _mk_inmaps_prefix(inputs)
    res = run_bass_kernel_spmd(nc, maps, core_ids=list(range(NCORES)))
    y = np.concatenate([res.results[i]["y"] for i in range(NCORES)], axis=0)
    return y.reshape(1, 8192, D).astype(np.float32)


def kernel_2launch(**inputs):
    nc = _get_nc("full")
    zero_s = np.zeros((NCORES * HD, D), dtype=np.float32)
    maps1 = _mk_inmaps(inputs, [zero_s] * NCORES)
    res1 = run_bass_kernel_spmd(nc, maps1, core_ids=list(range(NCORES)))
    s_all = np.concatenate([res1.results[i]["sloc"] for i in range(NCORES)], axis=0)
    s_all = np.ascontiguousarray(s_all)
    maps2 = _mk_inmaps(inputs, [s_all] * NCORES)
    res2 = run_bass_kernel_spmd(nc, maps2, core_ids=list(range(NCORES)))
    y = np.concatenate([res2.results[i]["y"] for i in range(NCORES)], axis=0)
    return y.reshape(1, 8192, D).astype(np.float32)


# version 09
# speedup vs baseline: 1.1206x; 1.1206x over previous
import numpy as np
from contextlib import ExitStack

import concourse.bass as bass
import concourse.mybir as mybir
from concourse.bass_utils import run_bass_kernel_spmd

F32 = mybir.dt.float32
BF16 = mybir.dt.bfloat16
AF = mybir.ActivationFunctionType
ALU = mybir.AluOpType
AX = mybir.AxisListType.X

NCORES = 8
T = 1024
NT = 8
D = 2048
KC = 16
H = 16
HD = 128
NCH = 8
HE = 1408
NJ = 11
EPS = 1e-5

ENGS = ("pe", "dv", "ac", "po", "sy")
SEMS = ("pe", "dv", "ac", "po", "sy", "st", "dst", "slc", "dxs",
        "ild", "isp", "isa", "irl",
        "sw0", "sw1", "sw2", "sw3", "sw4", "wd0", "wd1", "swr", "swg", "swb",
        "xpr", "shb", "shs", "sf0", "sf1", "pl1", "pl2", "shr", "flc", "se1", "se2")


class Plan:
    def __init__(self):
        self.thunks = {e: [] for e in ENGS}
        self.cnt = {s: 0 for s in SEMS}
        self.waited = {(e, s): 0 for e in ENGS for s in SEMS}
        self.sems = {}

    def ev(self, sem):
        return (sem, self.cnt[sem])

    def em(self, eng, fn, inc=None, by=1):
        if eng == inc:
            self.wait(eng, (eng, self.cnt[eng]), force=True)
        elif eng == "po" and inc is not None:
            self.wait("po", ("po", self.cnt["po"]), force=True)
        if inc is not None:
            self.cnt[inc] += by
            incs, incv = inc, by
        else:
            incs, incv = None, 0
        sems = self.sems

        def thunk(e):
            ins = fn(e)
            if incs is not None:
                ins.then_inc(sems[incs], incv)

        self.thunks[eng].append(thunk)
        return self.ev(inc) if inc is not None else None

    def wait(self, eng, *evs, force=False):
        best = {}
        for ev in evs:
            if ev is None:
                continue
            (s, v) = ev
            if (s == eng and not force) or v <= 0:
                continue
            if v > self.waited[(eng, s)] and v > best.get(s, 0):
                best[s] = v
        needed = []
        for s, v in best.items():
            needed.append((s, v))
            self.waited[(eng, s)] = v
        if not needed:
            return
        sems = self.sems

        def thunk(e):
            for (s, v) in needed:
                e.wait_ge(sems[s], v)

        self.thunks[eng].append(thunk)


NPRE = 6144
NPG = 6
MAGIC = 1234.5


def build(upto="full", dumps=(), prefix=False, detect=True):
    nc = bass.Bass("TRN2", target_bir_lowering=False,
                   detect_race_conditions=detect)

    def din(name, shape, dtype=F32):
        return nc.dram_tensor(name, list(shape), dtype, kind="ExternalInput").ap()

    x_in = din("x", (T, D))
    ldec = din("ldec", (1, H))
    pce = din("pce", (1, NCORES))
    nw_in = din("nw", (1, D))
    rw_in = din("rw", (8, D))
    wq_in = din("wq", (D, D)); wk_in = din("wk", (D, D))
    wv_in = din("wv", (D, D)); wo_in = din("wo", (D, D))
    wg1_in = din("wg1", (D, HD)); wg2_in = din("wg2", (HD, D))
    wgE = din("wgE", (8, D, HE)); wuE = din("wuE", (8, D, HE))
    wdE = din("wdE", (8, HE, D))
    sg_in = din("sg", (D, 2 * HE)); su_in = din("su", (D, 2 * HE))
    sd_in = din("sd", (2 * HE, D))
    s_all = din("s_all", (NCORES * HD, D))
    if prefix:
        xpre_in = din("xpre", (NPRE, D))
        wkh_in = din("wkh", (D, 1024))
        wvh_in = din("wvh", (D, 1024))
        ldh_in = din("ldh", (1, 8))
        mg_in = din("mg", (1, 3))
        slot_in = din("slot", (1, 1), mybir.dt.int32)
        pslot_in = din("pslot", (1, 1), mybir.dt.int32)
        sh = nc.dram_tensor("sh", [2 * 128, 3072], F32, addr_space="Shared").ap()
        shf = nc.dram_tensor("shf", [2, 16], F32, addr_space="Shared").ap()

    y_out = nc.dram_tensor("y", [T, D], F32, kind="ExternalOutput").ap()
    sloc_out = nc.dram_tensor("sloc", [HD, D], F32, kind="ExternalOutput").ap()
    xspill = nc.dram_tensor("xspill", [T, D], F32).ap()

    dump_t = {}
    for nm in dumps:
        dump_t[nm] = nc.dram_tensor(nm, [128, 16384], F32, kind="ExternalOutput").ap()
    n_dumps = len(dump_t)

    with ExitStack() as ctx:
        e_ = ctx.enter_context
        A = e_(nc.sbuf_tensor("A", [128, 16384], F32))
        B = e_(nc.sbuf_tensor("B", [128, 16384], BF16))
        C = e_(nc.sbuf_tensor("C", [128, 16384], BF16))
        W = e_(nc.sbuf_tensor("W", [128, 3 * 2048], BF16))
        SR = e_(nc.sbuf_tensor("SR", [128, 6144], F32))
        SB16 = e_(nc.sbuf_tensor("sbf", [128, 2048], BF16))
        LS = e_(nc.sbuf_tensor("lnscr", [128, 2048], BF16))
        VC = e_(nc.sbuf_tensor("vch", [128, 2048], BF16))
        KCH = e_(nc.sbuf_tensor("kpch", [128, 2048], BF16))
        DT = e_(nc.sbuf_tensor("DTt", [128, 2048], BF16))
        QD = e_(nc.sbuf_tensor("QDt", [128, 2048], BF16))
        NW = e_(nc.sbuf_tensor("NWb", [128, 2048], BF16))
        WG2 = e_(nc.sbuf_tensor("WG2s", [128, 2048], BF16))
        G1 = e_(nc.sbuf_tensor("G1f", [128, 1024], BF16))
        LDB = e_(nc.sbuf_tensor("LDB", [128, 16], F32))
        KDC = e_(nc.sbuf_tensor("KDC", [128, 16], F32))
        CDB = e_(nc.sbuf_tensor("CDB", [128, 16], F32))
        PCEb = e_(nc.sbuf_tensor("PCEb", [128, 8], F32))
        COEF = e_(nc.sbuf_tensor("COEF", [128, 128], F32))
        STT = e_(nc.sbuf_tensor("STTb", [128, 128], F32))
        RSTD = e_(nc.sbuf_tensor("RSTDb", [128, 128], F32))
        RT = e_(nc.sbuf_tensor("RTb", [128, 256], F32))
        G = e_(nc.sbuf_tensor("Gb", [128, 64], F32))
        RWT = e_(nc.sbuf_tensor("RWTb", [128, 128], BF16))
        IDT = e_(nc.sbuf_tensor("IDTb", [128, 128], BF16))
        ONE = e_(nc.sbuf_tensor("ONEb", [1, 128], F32))
        MS = e_(nc.sbuf_tensor("MSb", [128, 128], F32))
        KDF = e_(nc.sbuf_tensor("KDFb", [128, 2048], BF16))
        EPSC = e_(nc.sbuf_tensor("EPSC", [128, 1], F32))
        SLT = e_(nc.sbuf_tensor("SLT", [1, 2], mybir.dt.int32))
        FLG = e_(nc.sbuf_tensor("FLG", [1, 2], F32))

        PA = e_(nc.psum_tensor("PA", [128, 1024], F32))
        PB = e_(nc.psum_tensor("PB", [128, 1024], F32))
        PC = e_(nc.psum_tensor("PC", [128, 2048], F32))

        P = Plan()
        for s in SEMS:
            P.sems[s] = e_(nc.semaphore(s))

        Ab = A[:, :].bitcast(BF16)
        PCb = PC[:, :].bitcast(BF16)
        SRb = SR[:, :].bitcast(BF16)
        B3 = B[:, :].rearrange("p (k n) -> p k n", k=KC)
        C3 = C[:, :].rearrange("p (k n) -> p k n", k=KC)

        def xs(t):
            return A[:, t * 2048:(t + 1) * 2048]

        def qf(h, sl=None):
            ap = Ab[:, h * 1024:(h + 1) * 1024]
            return ap if sl is None else ap[:, sl]

        def kf(h, sl=None):
            ap = Ab[:, 16384 + h * 1024:16384 + (h + 1) * 1024]
            return ap if sl is None else ap[:, sl]

        def vf(h, sl=None):
            ap = C[:, h * 1024:(h + 1) * 1024]
            return ap if sl is None else ap[:, sl]

        def fmb(k, sl=None):
            ap = B[:, k * 1024:(k + 1) * 1024]
            return ap if sl is None else ap[:, sl]

        def ob(t):
            return B[:, t * 2048:(t + 1) * 2048]

        Ssb = SR[:, 0:2048]
        SIb = SR[:, 2048:4096]
        SJb = SR[:, 4096:6144]
        SCB = SJb.bitcast(BF16)
        TMPF = SR[:, 0:2048]

        wslot = [W[:, s * 2048:(s + 1) * 2048] for s in range(3)] + \
                [C[:, 11264:13312], C[:, 13312:15360]]
        ring_n = [3]

        def emit_x_load(sem):
            return P.em("sy", lambda e: e.dma_start(
                out=A[:, :].rearrange("p (t d) -> p t d", t=NT),
                in_=x_in.rearrange("(t p) d -> p t d", p=128)), inc=sem, by=16)

        if not prefix:
            emit_x_load("ild")
        P.em("sy", lambda e: e.dma_start(out=MS[0:1, 0:16], in_=ldec), inc="ild", by=16)
        P.em("sy", lambda e: e.dma_start(out=MS[0:1, 16:24], in_=pce),
             inc="ild", by=16)
        P.em("sy", lambda e: e.dma_start(out=TMPF[0:1, :], in_=nw_in),
             inc="ild", by=16)
        ILD = ("ild", 64 if not prefix else 80)
        ev_x = ILD
        ev_sm = ILD
        ev_nws = ILD
        RWS = LS
        ev_rw = P.em("po", lambda e: e.dma_start(out=RWS[0:8, :], in_=rw_in),
                     inc="swr", by=16)
        ev_wg2 = P.em("po", lambda e: e.dma_start(
            out=WG2[:, :].rearrange("p (g d) -> p g d", g=1),
            in_=wg2_in.rearrange("(g p) d -> p g d", p=128)), inc="swg", by=16)

        ev_one = P.em("dv", lambda e: e.memset(ONE[:, :], 1.0), inc="dv")
        ev_eps = P.em("dv", lambda e: e.memset(EPSC[:, :], EPS), inc="dv")
        ev_ms = P.em("po", lambda e: e.memset(IDT[:, :], 1.0), inc="po")
        P.wait("po", ev_ms, force=True)
        ev_idt = P.em("po", lambda e: e.affine_select(
            out=IDT[:, :], in_=IDT[:, :], pattern=[[1, 128]],
            compare_op=ALU.is_equal, fill=0.0, base=0, channel_multiplier=-1),
            inc="po")
        ev_io = P.em("po", lambda e: e.iota(
            COEF[:, :], pattern=[[1, 128]], base=0, channel_multiplier=-1,
            allow_small_or_imprecise_dtypes=True), inc="po")
        P.wait("po", ev_io, force=True)
        ev_td = P.em("po", lambda e: e.affine_select(
            out=COEF[:, :], in_=COEF[:, :], pattern=[[1, 128]],
            compare_op=ALU.is_ge, fill=1e9, base=0, channel_multiplier=-1),
            inc="po")
        ev_ti = P.em("po", lambda e: e.iota(
            RSTD[:, :], pattern=[[1, 128]], base=1, channel_multiplier=0,
            allow_small_or_imprecise_dtypes=True), inc="po")
        ev_tk = P.em("po", lambda e: e.iota(
            STT[:, 0:16], pattern=[[0, 16]], base=127, channel_multiplier=-1,
            allow_small_or_imprecise_dtypes=True), inc="po")

        P.wait("pe", ev_sm, ev_one)
        ev_b1 = P.em("pe", lambda e: e.matmul(PA[:, 0:16], ONE[0:1, :], MS[0:1, 0:16],
                                              start=True, stop=True))
        ev_b2 = P.em("pe", lambda e: e.matmul(PA[:, 16:24], ONE[0:1, :],
                                              MS[0:1, 16:24], start=True, stop=True),
                     inc="pe")
        P.wait("dv", ev_b2)
        ev_ldb = P.em("dv", lambda e: e.tensor_copy(LDB[:, :], PA[:, 0:16]), inc="dv")
        ev_pce = P.em("dv", lambda e: e.tensor_copy(PCEb[:, :], PA[:, 16:24]), inc="dv")

        P.wait("ac", ev_td, ev_ldb)
        for h in range(H):
            P.em("ac", lambda e, h=h: e.activation(
                DT[:, h * 128:(h + 1) * 128], COEF[:, :], AF.Exp,
                scale=LDB[:, h:h + 1]), inc="ac")
        ev_dt = P.ev("ac")
        def build_qd(extra_wait=None, redo_iota=False):
            if redo_iota:
                P.wait("po", extra_wait, ("ac", P.cnt["ac"]))
                ev_ti2 = P.em("po", lambda e: e.iota(
                    RSTD[:, :], pattern=[[1, 128]], base=1, channel_multiplier=0,
                    allow_small_or_imprecise_dtypes=True), inc="po")
                P.wait("ac", ev_ti2, extra_wait)
            else:
                P.wait("ac", ev_ti, extra_wait)
            for h in range(H):
                P.em("ac", lambda e, h=h: e.activation(
                    QD[:, h * 128:(h + 1) * 128], RSTD[:, :], AF.Exp,
                    scale=LDB[:, h:h + 1]), inc="ac")
            return P.ev("ac")

        if not prefix:
            ev_qd = build_qd()
        else:
            ev_qd = None
        P.wait("dv", ev_tk, ev_ldb)
        ev_kd0 = P.em("dv", lambda e: e.tensor_tensor(
            STT[:, 16:32], STT[:, 0:16], LDB[:, :], ALU.mult), inc="dv")
        P.wait("ac", ev_kd0)
        ev_kdc = P.em("ac", lambda e: e.activation(KDC[:, :], STT[:, 16:32], AF.Exp),
                      inc="ac")
        P.wait("po", ev_qd if ev_qd is not None else ("ac", P.cnt["ac"]))
        ev_tk2 = P.em("po", lambda e: e.iota(
            RSTD[:, :], pattern=[[0, 128]], base=127, channel_multiplier=-1,
            allow_small_or_imprecise_dtypes=True), inc="po")
        P.wait("ac", ev_tk2)
        for h in range(H):
            P.em("ac", lambda e, h=h: e.activation(
                KDF[:, h * 128:(h + 1) * 128], RSTD[:, :], AF.Exp,
                scale=LDB[:, h:h + 1]), inc="ac")
        ev_kdf = P.ev("ac")
        ev_cdb = P.em("ac", lambda e: e.activation(CDB[:, :], LDB[:, :], AF.Exp,
                                                   scale=128.0), inc="ac")
        if not prefix:
            P.wait("dv", ev_pce, ev_qd, ev_dt)
            for j in range(NCORES):
                P.em("dv", lambda e, j=j: e.tensor_scalar(
                    COEF[:, j * 16:(j + 1) * 16], LDB[:, :], PCEb[:, j:j + 1],
                    None, ALU.mult), inc="dv")
            ev_cf0 = P.ev("dv")
            P.wait("ac", ev_cf0)
            ev_coef = P.em("ac", lambda e: e.activation(
                COEF[:, :], COEF[:, :], AF.Exp), inc="ac")
        else:
            P.em("sy", lambda e: e.dma_start(out=MS[0:1, 24:32], in_=ldh_in),
                 inc="ild", by=16)
            P.em("sy", lambda e: e.dma_start(out=MS[0:1, 28:31][:, 0:0]
                                             if False else MS[0:1, 96:99],
                                             in_=mg_in), inc="ild", by=16)
            P.em("sy", lambda e: e.dma_start(out=SLT[0:1, 0:1], in_=slot_in),
                 inc="shr", by=16)
            P.em("sy", lambda e: e.dma_start(out=SLT[0:1, 1:2], in_=pslot_in),
                 inc="shr", by=16)
            P.wait("pe", ("ild", 80))
            P.em("pe", lambda e: e.matmul(PB[:, 0:8], ONE[0:1, :], MS[0:1, 24:32],
                                          start=True, stop=True))
            ev_bc3 = P.em("pe", lambda e: e.matmul(
                PB[:, 8:11], ONE[0:1, :], MS[0:1, 96:99],
                start=True, stop=True), inc="pe")
            P.wait("dv", ev_bc3)
            ev_ldh = P.em("dv", lambda e: e.tensor_copy(MS[:, 96:104], PB[:, 0:8]),
                          inc="dv")
            ev_mgb = P.em("dv", lambda e: e.tensor_copy(MS[:, 120:123], PB[:, 8:11]),
                          inc="dv")
            P.wait("dv", ev_tk)
            ev_kp0 = P.em("dv", lambda e: e.tensor_tensor(
                MS[:, 104:112], STT[:, 0:8], MS[:, 96:104], ALU.mult), inc="dv")
            P.wait("ac", ev_kp0)
            ev_kpre = P.em("ac", lambda e: e.activation(
                MS[:, 104:112], MS[:, 104:112], AF.Exp), inc="ac")
            ev_cpre = P.em("ac", lambda e: e.activation(
                MS[:, 112:120], MS[:, 96:104], AF.Exp, scale=128.0), inc="ac")
            P.wait("ac", ev_tk2)
            for h in range(8):
                P.em("ac", lambda e, h=h: e.activation(
                    QD[:, h * 128:(h + 1) * 128], RSTD[:, :], AF.Exp,
                    scale=MS[:, 96 + h:97 + h]), inc="ac")
            ev_kdfpre = P.ev("ac")
            P.wait("dv", ev_dt, ev_mgb)
            for j in range(3):
                P.em("dv", lambda e, j=j: e.tensor_scalar(
                    COEF[:, j * 16:(j + 1) * 16], LDB[:, :],
                    MS[:, 120 + j:121 + j], None, ALU.mult), inc="dv")
            ev_cf0 = P.ev("dv")
            P.wait("ac", ev_cf0)
            ev_coef = P.em("ac", lambda e: e.activation(
                COEF[:, 0:48], COEF[:, 0:48], AF.Exp), inc="ac")
            P.wait("dv", ev_coef)
            ev_c1n = P.em("dv", lambda e: e.tensor_scalar(
                COEF[:, 48:64], COEF[:, 32:48], -1.0, None, ALU.mult), inc="dv")

        P.wait("pe", ev_nws)
        for c4 in range(4):
            ev_nwb = P.em("pe", lambda e, c4=c4: e.matmul(
                PC[:, c4 * 512:(c4 + 1) * 512], ONE[0:1, :],
                TMPF[0:1, c4 * 512:(c4 + 1) * 512], start=True, stop=True),
                inc="pe" if c4 == 3 else None)
        P.wait("dv", ev_nwb)
        ev_nw2 = P.em("dv", lambda e: e.tensor_copy(NW[:, :], PC[:, :]), inc="dv")

        P.wait("pe", ev_rw, ev_idt, ev_nw2)
        for k in range(KC):
            ev_rt0 = P.em("pe", lambda e, k=k: e.transpose(
                PCb[:, k * 8:(k + 1) * 8], RWS[0:8, k * 128:(k + 1) * 128],
                IDT[0:8, 0:8]), inc="pe" if k == KC - 1 else None)
        P.wait("dv", ev_rt0)
        ev_rwt = P.em("dv", lambda e: e.tensor_copy(RWT[:, :], PCb[:, 0:128]),
                      inc="dv")

        def ln_stats_maker(src_fn, sb, extra_ac_wait=None, ac_wait=None,
                           dv_wait=None):
            ev_sq = {}
            ev_rstd_t = {}

            def step(t):
                if t < NT:
                    if t == 0:
                        P.wait("ac", extra_ac_wait, ac_wait, dv_wait)
                    src = src_fn(t)
                    P.em("ac", lambda e, src=src, t=t: e.activation(
                        TMPF[:, :], src, AF.Copy,
                        accum_out=sb[:, 32 + t:33 + t]), inc="ac")
                    ev_sq[t] = P.em("ac", lambda e, src=src, t=t: e.activation(
                        TMPF[:, :], src, AF.Square,
                        accum_out=sb[:, 40 + t:41 + t]), inc="ac")
                tp = t - 1
                if 0 <= tp < NT:
                    P.wait("dv", ev_sq[tp])
                    P.em("dv", lambda e, t=tp: e.tensor_scalar(
                        sb[:, 48 + t:49 + t], sb[:, 32 + t:33 + t], -1.0 / D,
                        None, ALU.mult), inc="dv")
                    P.em("dv", lambda e, t=tp: e.tensor_tensor(
                        sb[:, 56 + t:57 + t], sb[:, 48 + t:49 + t],
                        sb[:, 48 + t:49 + t], ALU.mult), inc="dv")
                    ev_var = P.em("dv", lambda e, t=tp: e.scalar_tensor_tensor(
                        sb[:, 64 + t:65 + t], sb[:, 40 + t:41 + t], 1.0 / D,
                        sb[:, 56 + t:57 + t], ALU.mult, ALU.subtract),
                        inc="dv")
                    P.wait("ac", ev_var)
                    P.em("ac", lambda e, t=tp: e.activation(
                        sb[:, 72 + t:73 + t], sb[:, 64 + t:65 + t], AF.Ln,
                        bias=EPSC[:, 0:1]), inc="ac")
                    ev_rstd_t[tp] = P.em("ac", lambda e, t=tp: e.activation(
                        sb[:, 80 + t:81 + t], sb[:, 72 + t:73 + t], AF.Exp,
                        scale=-0.5), inc="ac")
            return step, ev_rstd_t

        def ln_apply_maker(src_fn, dst3, sb, ev_rstd_t, apply_wait=None):
            read_done = {}
            tr_evs = {}
            fm_evs = {}

            def astep(tq):
                if not (0 <= tq < NT):
                    return
                src = src_fn(tq)
                buf = LS if tq % 2 == 0 else VC
                half = (tq % 2) * 2048
                P.wait("dv", ev_rstd_t[tq], tr_evs.get(tq - 2),
                       apply_wait if tq <= 1 else None)
                ev_cp = P.em("dv", lambda e, src=src, t=tq, buf=buf:
                             e.tensor_scalar(
                                 buf[:, :], src, sb[:, 48 + t:49 + t],
                                 sb[:, 80 + t:81 + t], ALU.add, ALU.mult),
                             inc="dv")
                read_done[tq] = ev_cp
                P.wait("pe", ev_cp, fm_evs.get(tq - 2))
                for k in range(KC):
                    ev_tr = P.em("pe", lambda e, k=k, buf=buf, half=half:
                                 e.transpose(
                                     PCb[:, half + k * 128:half + (k + 1) * 128],
                                     buf[:, k * 128:(k + 1) * 128], IDT[:, :]),
                                 inc="pe" if k == KC - 1 else None)
                tr_evs[tq] = ev_tr
                P.wait("ac", ev_tr)
                fm_evs[tq] = P.em("ac", lambda e, tq=tq, half=half: e.copy(
                    dst3[:, :, tq * 128:(tq + 1) * 128],
                    PCb[:, half:half + 2048].rearrange("p (k n) -> p k n", k=KC)),
                    inc="ac")
            return astep, read_done, fm_evs

        def ln_apply(src_fn, dst3, sb, ev_rstd_t, apply_wait=None):
            astep, read_done, fm_evs = ln_apply_maker(
                src_fn, dst3, sb, ev_rstd_t, apply_wait=apply_wait)
            for tq in range(NT):
                astep(tq)
            return [read_done[t] for t in range(NT)], fm_evs[NT - 1]

        def emit_ln(src_fn, dst3, extra_ac_wait=None, sb=None, apply_wait=None,
                    ac_wait=None):
            sbu = MS if sb is None else sb
            step, ev_rstd_t = ln_stats_maker(src_fn, sbu,
                                             extra_ac_wait=extra_ac_wait,
                                             ac_wait=ac_wait)
            astep, read_done, fm_evs = ln_apply_maker(
                src_fn, dst3, sbu, ev_rstd_t, apply_wait=apply_wait)
            for t in range(NT + 3):
                if t <= NT:
                    step(t)
                astep(t - 2)
            return [read_done[t] for t in range(NT)], fm_evs[NT - 1]

        ring_owner = [None, None, None, None, None]
        slot_ctr = [0]

        def load_wslot(src3, waits=()):
            s = slot_ctr[0] % ring_n[0]
            slot_ctr[0] += 1
            P.wait("po", ring_owner[s], *waits)
            ev = P.em("po", lambda e, s=s, src3=src3: e.dma_start(
                out=wslot[s][:, :].rearrange("p (a b) -> p a b", a=src3.shape[1]),
                in_=src3), inc=f"sw{s}", by=16)
            return s, ev

        def eblk(w_ap, e0, ew=128):
            return w_ap.rearrange("(k p) e -> p k e", p=128)[:, :, e0:e0 + ew]

        proj_war = {0: ev_pce, 1: ev_pce}

        def proj(w_ap, dst_fn, act, n_eb=KC):
            for ebi in range(n_eb):
                s, ev_l = load_wslot(eblk(w_ap, ebi * 128))
                par = ebi % 2
                ps = PA if par == 0 else PB
                P.wait("pe", ev_l, proj_war[par])
                for k in range(KC):
                    P.em("pe", lambda e, s=s, k=k, ps=ps: e.matmul(
                        ps[:, 0:512], wslot[s][:, k * 128:(k + 1) * 128],
                        fmb(k, slice(0, 512)),
                        start=(k == 0), stop=(k == KC - 1)))
                    ev_mm = P.em("pe", lambda e, s=s, k=k, ps=ps: e.matmul(
                        ps[:, 512:1024], wslot[s][:, k * 128:(k + 1) * 128],
                        fmb(k, slice(512, 1024)),
                        start=(k == 0), stop=(k == KC - 1)),
                        inc="pe" if k == KC - 1 else None)
                ring_owner[s] = ev_mm
                P.wait("ac", ev_mm)
                dst = dst_fn(ebi)
                if act is None:
                    ev_e = P.em("ac", lambda e, ps=ps, dst=dst: e.copy(dst, ps[:, :]),
                                inc="ac")
                else:
                    ev_e = P.em("ac", lambda e, ps=ps, dst=dst, act=act: e.activation(
                        dst, ps[:, :], act), inc="ac")
                proj_war[par] = ev_e
            return P.ev("ac")

        if prefix:
            xp3 = xpre_in.rearrange("(g t p) d -> p g t d", p=128, t=NT)
            ev_xg = {}
            ev_xg[0] = P.em("sy", lambda e: e.dma_start(
                out=A[:, :].rearrange("p (t d) -> p t d", t=NT),
                in_=xp3[:, 0]), inc="xpr", by=16)
            P.wait("po", ("shr", 32))
            P.em("po", lambda e: e.memset(FLG[:, :], 0.0), inc="po")
            rs = None
            rp = None

            def load_rs(e):
                return e.value_load(SLT[0:1, 0:1])

            def load_rp(e):
                return e.value_load(SLT[0:1, 1:2])

            def th_clear(e):
                r = load_rs(e)
                return e.dma_start(out=shf[bass.ts(r, 1), 0:2], in_=FLG[0:1, 0:2])
            P.em("po", th_clear, inc="flc", by=16)
            P.wait("po", ("flc", 16))
            P.em("po", lambda e: e.memset(FLG[0:1, 1:2], MAGIC), inc="po")

            ev_bh0 = P.em("dv", lambda e: e.memset(SJb[:, 0:1024], 0.0), inc="dv")
            ev_bh = ev_bh0
            ev_pvc = None
            last_buf_apply = {0: None, 1: None}
            step0, rstd0 = ln_stats_maker(
                xs, MS, extra_ac_wait=ev_nwb, ac_wait=ev_xg[0],
                dv_wait=ev_xg[0])
            P.wait("ac", ev_eps)
            for t in range(NT + 1):
                step0(t)
            stats_cur = (MS, rstd0)
            for g in range(NPG):
                pe_guard = P.ev("pe")
                P.wait("pe", ev_bh, ev_pvc)
                if g == 0:
                    P.wait("pe", ev_rwt)
                    P.wait("dv", ev_rt0)
                sb_g, rstd_g = stats_cur
                png_read, ev_plng = ln_apply(xs, B3, sb_g, rstd_g,
                                             apply_wait=pe_guard)
                last_buf_apply[g % 2] = png_read[-1]
                P.wait("sy", *png_read)
                if g + 1 < NPG:
                    ev_nx = ev_xg[g + 1] = P.em("sy", lambda e, g=g: e.dma_start(
                        out=A[:, :].rearrange("p (t d) -> p t d", t=NT),
                        in_=xp3[:, g + 1]), inc="xpr", by=16)
                else:
                    ev_nx = ev_x_main = emit_x_load("xpr")
                P.wait("pe", ev_plng)
                ev_kh = proj(wkh_in, lambda eb: C[:, eb * 1024:(eb + 1) * 1024],
                             AF.Silu, n_eb=8)
                ev_vh = proj(wvh_in,
                             lambda eb: C[:, 8192 + eb * 1024:8192 + (eb + 1) * 1024],
                             AF.Silu, n_eb=8)
                sb_n = STT if g % 2 == 0 else MS
                stepn, rstdn = ln_stats_maker(
                    xs, sb_n, ac_wait=last_buf_apply[(g + 1) % 2],
                    dv_wait=ev_nx)
                P.wait("ac", ev_nx)
                stats_cur = (sb_n, rstdn)
                pkd_ev = {}
                pvc_ev = {}
                umm_ev = {}
                bh_ev = {}
                for c in range(NCH + 1):
                    if c < NCH:
                        csl = slice(c * 128, (c + 1) * 128)
                        half = (c % 2) * 2048
                        kh0 = (c % 2) * 1024
                        P.wait("pe", ev_kh if c < 2 else None,
                               ev_vh if c < 2 else None,
                               ev_kdfpre if (g == 0 and c == 0) else None,
                               pkd_ev.get(c - 2), pvc_ev.get(c - 2))
                        for h in range(8):
                            P.em("pe", lambda e, h=h, csl=csl, half=half:
                                 e.transpose(
                                     PCb[:, half + h * 128:half + (h + 1) * 128],
                                     C[:, h * 1024:(h + 1) * 1024][:, csl],
                                     IDT[:, :]))
                        for h in range(8):
                            ev_ptr = P.em("pe", lambda e, h=h, csl=csl,
                                          half=half: e.transpose(
                                              PCb[:, half + 1024 + h * 128:half + 1024 + (h + 1) * 128],
                                              C[:, 8192 + h * 1024:8192 + (h + 1) * 1024][:, csl],
                                              IDT[:, :]),
                                          inc="pe" if h == 7 else None)
                        P.wait("dv", ev_ptr, umm_ev.get(c - 2))
                        ev_pkd = pkd_ev[c] = P.em(
                            "dv", lambda e, half=half, kh0=kh0: e.tensor_tensor(
                                KCH[:, kh0:kh0 + 1024],
                                PCb[:, half:half + 1024],
                                QD[:, 0:1024], ALU.mult), inc="dv")
                        P.wait("ac", ev_ptr, umm_ev.get(c - 2))
                        ev_pvc = pvc_ev[c] = P.em(
                            "ac", lambda e, half=half, kh0=kh0: e.copy(
                                VC[:, kh0:kh0 + 1024],
                                PCb[:, half + 1024:half + 2048]), inc="ac")
                    cp = c - 1
                    if 0 <= cp < NCH:
                        kh0p = (cp % 2) * 1024
                        ps_u = PA if cp % 2 == 0 else PB
                        P.wait("pe", pkd_ev[cp], pvc_ev[cp],
                               ev_vh if cp < 2 else None, bh_ev.get(cp - 2))
                        for h in range(8):
                            ev_umm = P.em("pe", lambda e, h=h, ps_u=ps_u,
                                          kh0=kh0p: e.matmul(
                                              ps_u[:, h * 128:(h + 1) * 128],
                                              KCH[:, kh0 + h * 128:kh0 + (h + 1) * 128],
                                              VC[:, kh0 + h * 128:kh0 + (h + 1) * 128],
                                              start=True, stop=True),
                                          inc="pe" if h == 7 else None)
                        umm_ev[cp] = ev_umm
                        P.wait("dv", ev_umm, ev_kpre, ev_cpre)
                        P.em("dv", lambda e: e.tensor_tensor(
                            SJb[:, 0:1024].rearrange("p (h n) -> p h n", h=8),
                            SJb[:, 0:1024].rearrange("p (h n) -> p h n", h=8),
                            MS[:, 112:120].rearrange("p (h o) -> p h o", o=1)
                            .broadcast_to([128, 8, 128]), ALU.mult), inc="dv")
                        P.em("dv", lambda e, ps_u=ps_u: e.tensor_tensor(
                            SJb[:, 0:1024], SJb[:, 0:1024], ps_u[:, 0:1024],
                            ALU.add), inc="dv")
                        ev_bh = bh_ev[cp] = P.ev("dv")
                        stepn(cp)
                stepn(NCH)
                proj_war[0] = bh_ev[NCH - 2]
                proj_war[1] = bh_ev[NCH - 1]
            def th_post_bh(e):
                r = load_rs(e)
                return e.dma_start(out=sh[bass.ts(r, 128), 0:1024],
                                  in_=SJb[:, 0:1024])
            P.wait("po", ev_bh)
            P.em("po", th_post_bh, inc="shb", by=16)
            P.wait("po", ("shb", 16))

            def th_post_f1(e):
                r = load_rs(e)
                return e.dma_start(out=shf[bass.ts(r, 1), 0:1], in_=FLG[0:1, 1:2])
            P.em("po", th_post_f1, inc="sf0", by=16)

            ev_qd = build_qd(extra_wait=ev_pkd, redo_iota=True)

            def th_poll1(e):
                with e.register("cnt") as cnt:
                    e.reg_mov(cnt, 0)

                    def cond():
                        return e.scalar_reg_alu(
                            ALU.not_equal,
                            e.value_load(FLG[0:1, 0:1].bitcast(mybir.dt.int32)),
                            int(np.float32(MAGIC).view(np.int32)))

                    rp_ = load_rp(e)
                    with e.While(cond):
                        e.reg_add(cnt, cnt, 16)
                        e.dma_start(out=FLG[0:1, 0:1],
                                    in_=shf[bass.ts(rp_, 1), 0:1]).then_inc(
                                        P.sems["pl1"], 16)
                        e.wait_ge(P.sems["pl1"], cnt)
                return e.nop()
            P.em("po", th_poll1, inc="po")
            P.em("po", lambda e: e.dma_start(out=SIb[:, 0:1024],
                                             in_=sh[0:128, 0:1024]),
                 inc="se1", by=16)
            P.em("po", lambda e: e.dma_start(out=SIb[:, 1024:2048],
                                             in_=sh[128:256, 0:1024]),
                 inc="se1", by=16)
            ev_e1 = ("se1", 32)
            ev_x = ("xpr", (NPG + 1) * 16)
        else:
            ev_x_main = None

        if prefix:
            pe_guard1 = P.ev("pe")
            sb1, rstd1 = stats_cur
            ln1_read, ev_ln1 = ln_apply(xs, B3, sb1, rstd1,
                                        apply_wait=pe_guard1)
        else:
            P.wait("dv", ev_x)
            P.wait("ac", ev_x, ev_eps)
            P.wait("pe", ev_rwt, ev_x)
            P.wait("dv", ev_rt0)
            ln1_read, ev_ln1 = emit_ln(xs, B3, extra_ac_wait=ev_nwb)

        for t in [4, 5, 6, 7, 0, 1, 2, 3]:
            P.wait("sy", ln1_read[t])
            P.em("sy", lambda e, t=t: e.dma_start(
                out=xspill[t * 128:(t + 1) * 128, :], in_=xs(t)), inc="isp", by=16)
        SPILLS = ("isp", 128)
        spill_ev = {t: SPILLS for t in range(NT)}

        P.wait("pe", ev_ln1, spill_ev[4], spill_ev[5], spill_ev[6], spill_ev[7])
        ev_k = proj(wk_in, lambda hh: kf(hh), AF.Silu)
        ev_v = proj(wv_in, lambda hh: vf(hh), AF.Silu)
        P.wait("pe", spill_ev[0], spill_ev[1], spill_ev[2], spill_ev[3])
        ev_q = proj(wq_in, lambda hh: qf(hh), AF.Silu)
        ev_g1 = proj(wg1_in, lambda _: G1[:, :], None, n_eb=1)

        if prefix:
            P.wait("dv", ev_e1, ev_coef)
            for h in range(H):
                P.em("dv", lambda e, h=h: e.tensor_scalar(
                    Ssb[:, h * 128:(h + 1) * 128],
                    SIb[:, h * 128:(h + 1) * 128],
                    COEF[:, h:h + 1], None, ALU.mult), inc="dv")
            ev_sinit = P.ev("dv")
            ev_scp = ev_sinit
        else:
            ev_si0 = P.em("dv", lambda e: e.memset(SIb, 0.0), inc="dv")
            prev_sij = None
            for j in range(NCORES):
                P.wait("sy", prev_sij)
                ev_sj = P.em("sy", lambda e, j=j: e.dma_start(
                    out=SJb, in_=s_all[j * 128:(j + 1) * 128, :]), inc="isa", by=16)
                P.wait("dv", ev_sj, ev_coef)
                for h in range(H):
                    P.em("dv", lambda e, j=j, h=h: e.scalar_tensor_tensor(
                        SIb[:, h * 128:(h + 1) * 128],
                        SJb[:, h * 128:(h + 1) * 128],
                        COEF[:, j * 16 + h:j * 16 + h + 1],
                        SIb[:, h * 128:(h + 1) * 128], ALU.mult, ALU.add),
                        inc="dv")
                prev_sij = P.ev("dv")
            ev_sinit = P.ev("dv")

        if not prefix:
            ev_scp = P.em("dv", lambda e: e.tensor_copy(Ssb, SIb), inc="dv")
        ev_oev = None
        ev_supd = None
        ev_umm_prev = None
        for c in range(NCH):
            csl = slice(c * 128, (c + 1) * 128)
            P.wait("pe", ev_oev, ev_k if c == 0 else None, ev_v if c == 0 else None,
                   ev_q if c == 0 else None, ev_g1 if c == 0 else None)
            for h in range(H):
                P.em("pe", lambda e, h=h, csl=csl: e.transpose(
                    PCb[:, h * 128:(h + 1) * 128], kf(h, csl), IDT[:, :]))
            for h in range(H):
                ev_tr = P.em("pe", lambda e, h=h, csl=csl: e.transpose(
                    PCb[:, 2048 + h * 128:2048 + (h + 1) * 128], vf(h, csl),
                    IDT[:, :]), inc="pe" if h == H - 1 else None)
            P.wait("dv", ev_tr, ev_umm_prev, ev_scp if c == 0 else None,
                   ev_sinit if c == 0 else None, ev_q if c == 0 else None,
                   ev_kdf if c == 0 else None)
            ev_kp = P.em("dv", lambda e: e.tensor_tensor(
                KCH[:, :], PCb[:, 0:2048], KDF[:, :], ALU.mult), inc="dv")
            ev_qp = P.em("dv", lambda e, c=c: e.tensor_tensor(
                LS[:, :].rearrange("p (h n) -> p h n", h=H),
                Ab[:, 0:16384].rearrange("p (h n) -> p h n", h=H)[:, :, c * 128:(c + 1) * 128],
                QD[:, :].rearrange("p (h n) -> p h n", h=H), ALU.mult), inc="dv")
            P.wait("ac", ev_tr, ev_umm_prev, ev_supd,
                   ev_scp if c == 0 else None, ev_sinit if c == 0 else None)
            ev_vch = P.em("ac", lambda e: e.copy(VC[:, :], PCb[:, 2048:4096]),
                          inc="ac")
            ev_sbf = P.em("ac", lambda e: e.copy(SB16[:, :], Ssb), inc="ac")
            P.wait("pe", ev_qp, ev_sbf)
            for h in range(H):
                ps = PA if h < 8 else PB
                off = (h % 8) * 128
                ev_sc = P.em("pe", lambda e, h=h, ps=ps, off=off, csl=csl: e.matmul(
                    ps[:, off:off + 128], kf(h, csl), qf(h, csl),
                    start=True, stop=True), inc="pe" if h == H - 1 else None)
            P.wait("dv", ev_sc, ("shb", 16) if (prefix and c == 0) else None)
            P.em("dv", lambda e: e.tensor_tensor(
                SCB[:, 0:1024], PA[:, :], DT[:, 0:1024], ALU.mult), inc="dv")
            ev_scb = P.em("dv", lambda e: e.tensor_tensor(
                SCB[:, 1024:2048], PB[:, :], DT[:, 1024:2048], ALU.mult), inc="dv")
            P.wait("pe", ev_scb)
            for h in range(H):
                hsl = slice(h * 128, (h + 1) * 128)
                P.em("pe", lambda e, hsl=hsl: e.matmul(
                    PC[:, hsl], SCB[:, hsl], VC[:, hsl], start=True, stop=False))
                ev_omm = P.em("pe", lambda e, hsl=hsl: e.matmul(
                    PC[:, hsl], LS[:, hsl], SB16[:, hsl], start=False, stop=True),
                    inc="pe" if h == H - 1 else None)
            for h in range(H):
                ps = PA if h < 8 else PB
                off = (h % 8) * 128
                hsl = slice(h * 128, (h + 1) * 128)
                ev_umm = P.em("pe", lambda e, ps=ps, off=off, hsl=hsl: e.matmul(
                    ps[:, off:off + 128], KCH[:, hsl], VC[:, hsl],
                    start=True, stop=True), inc="pe" if h == H - 1 else None)
            ev_umm_prev = ev_umm
            P.wait("dv", ev_umm)
            P.em("dv", lambda e: e.tensor_tensor(
                Ssb[:, :].rearrange("p (h n) -> p h n", h=H),
                Ssb[:, :].rearrange("p (h n) -> p h n", h=H),
                CDB[:, :].rearrange("p (h o) -> p h o", o=1)
                .broadcast_to([128, H, 128]), ALU.mult), inc="dv")
            P.em("dv", lambda e: e.tensor_tensor(
                Ssb[:, 0:1024], Ssb[:, 0:1024], PA[:, :], ALU.add), inc="dv")
            P.em("dv", lambda e: e.tensor_tensor(
                Ssb[:, 1024:2048], Ssb[:, 1024:2048], PB[:, :], ALU.add),
                inc="dv")
            ev_supd = P.ev("dv")
            P.wait("ac", ev_omm)
            ev_oev = P.em("ac", lambda e, c=c: e.copy(ob(c), PC[:, :]), inc="ac")

        ev_scan = ev_oev
        P.wait("sy", ev_supd)
        ev_sloc = P.em("sy", lambda e: e.dma_start(out=sloc_out, in_=Ssb),
                       inc="slc", by=16)

        if prefix:
            def th_post_s(e):
                r = e.value_load(SLT[0:1, 0:1])
                return e.dma_start(out=sh[bass.ts(r, 128), 1024:3072], in_=Ssb)
            P.wait("po", ev_supd)
            P.em("po", th_post_s, inc="shs", by=16)
            P.wait("po", ("shs", 16))

            def th_post_f2(e):
                r = e.value_load(SLT[0:1, 0:1])
                return e.dma_start(out=shf[bass.ts(r, 1), 1:2], in_=FLG[0:1, 1:2])
            P.em("po", th_post_f2, inc="sf1", by=16)
            P.em("po", lambda e: e.memset(FLG[0:1, 0:1], 0.0), inc="po")

            qp2_hoist = {}
            for th in range(2):
                qbuf_h = KCH if th % 2 == 0 else KDF
                qp2_hoist[th] = P.em("po", lambda e, t=th, qbuf=qbuf_h:
                                     e.tensor_tensor(
                                         qbuf[:, :].rearrange("p (h n) -> p h n", h=H),
                                         Ab[:, 0:16384].rearrange("p (h n) -> p h n", h=H)[:, :, t * 128:(t + 1) * 128],
                                         QD[:, :].rearrange("p (h n) -> p h n", h=H),
                                         ALU.mult), inc="po")

            def th_poll2(e):
                with e.register("cnt2") as cnt:
                    e.reg_mov(cnt, 0)

                    def cond():
                        return e.scalar_reg_alu(
                            ALU.not_equal,
                            e.value_load(FLG[0:1, 0:1].bitcast(mybir.dt.int32)),
                            int(np.float32(MAGIC).view(np.int32)))

                    rp_ = e.value_load(SLT[0:1, 1:2])
                    with e.While(cond):
                        e.reg_add(cnt, cnt, 16)
                        e.dma_start(out=FLG[0:1, 0:1],
                                    in_=shf[bass.ts(rp_, 1), 1:2]).then_inc(
                                        P.sems["pl2"], 16)
                        e.wait_ge(P.sems["pl2"], cnt)
                return e.nop()
            P.em("po", th_poll2, inc="po")

            def th_read_s(e):
                r = e.value_load(SLT[0:1, 1:2])
                return e.dma_start(out=SJb, in_=sh[bass.ts(r, 128), 1024:3072])
            P.em("po", th_read_s, inc="se2", by=16)

            P.wait("dv", ("se2", 16), ev_c1n)
            for h in range(H):
                P.em("dv", lambda e, h=h: e.tensor_scalar(
                    SJb[:, h * 128:(h + 1) * 128], SJb[:, h * 128:(h + 1) * 128],
                    COEF[:, 16 + h:17 + h], None, ALU.mult), inc="dv")
            for h in range(H):
                P.em("dv", lambda e, h=h: e.scalar_tensor_tensor(
                    SJb[:, h * 128:(h + 1) * 128], SIb[:, h * 128:(h + 1) * 128],
                    COEF[:, 48 + h:49 + h], SJb[:, h * 128:(h + 1) * 128],
                    ALU.mult, ALU.add), inc="dv")
            ev_dS = P.ev("dv")

        ev_dmp = []
        if "d_ob" in dump_t:
            P.wait("po", ev_scan)
            ev_dmp.append(P.em("po", lambda e: e.dma_start(
                out=dump_t["d_ob"], in_=B[:, :]), inc="dst", by=16))
        if "d_q" in dump_t:
            P.wait("po", ev_q)
            ev_dmp.append(P.em("po", lambda e: e.dma_start(
                out=dump_t["d_q"], in_=Ab[:, 0:16384]), inc="dst", by=16))
        if "d_k" in dump_t:
            P.wait("po", ev_k)
            ev_dmp.append(P.em("po", lambda e: e.dma_start(
                out=dump_t["d_k"], in_=Ab[:, 16384:32768]), inc="dst", by=16))
        if "d_v" in dump_t:
            P.wait("po", ev_v)
            ev_dmp.append(P.em("po", lambda e: e.dma_start(
                out=dump_t["d_v"], in_=C[:, :]), inc="dst", by=16))

        def emit_reload():
            return P.em("sy", lambda e: e.dma_start(
                out=A[:, :].rearrange("p (t d) -> p t d", t=NT),
                in_=xspill.rearrange("(t p) d -> p t d", p=128)), inc="irl", by=16)

        P.wait("sy", ev_scan, *ev_dmp)
        if not prefix:
            ev_rel = emit_reload()

        P.wait("pe", ev_supd, ev_scan, ev_wg2, ev_g1)
        if prefix:
            P.wait("ac", ("shs", 16))
        P.wait("dv", *ev_dmp)
        P.wait("ac", ("slc", 16))
        sig_ev = {}
        obg_ev = {}
        cadd_ev = {}
        cmm_ev = {}
        ds16_ev = {}
        qp2_ev = {}
        FAC = COEF[:, 64:80]
        for t in range(NT + 2):
            if prefix:
                if t < NT:
                    qbuf = KCH if t % 2 == 0 else KDF
                    dbuf = SB16 if t % 2 == 0 else DT
                    if t < 2:
                        qp2_ev[t] = qp2_hoist[t]
                    else:
                        P.wait("po", cmm_ev.get(t - 2))
                        qp2_ev[t] = P.em("po", lambda e, t=t, qbuf=qbuf:
                                         e.tensor_tensor(
                                             qbuf[:, :].rearrange("p (h n) -> p h n", h=H),
                                             Ab[:, 0:16384].rearrange("p (h n) -> p h n", h=H)[:, :, t * 128:(t + 1) * 128],
                                             QD[:, :].rearrange("p (h n) -> p h n", h=H),
                                             ALU.mult), inc="po")
                    P.wait("dv", cmm_ev.get(t - 2))
                    if t == 0:
                        ds16_ev[0] = P.em("dv", lambda e: e.tensor_copy(
                            SB16[:, :], SJb), inc="dv")
                        P.em("dv", lambda e: e.tensor_copy(
                            FAC[:, :], CDB[:, :]), inc="dv")
                    else:
                        ds16_ev[t] = P.em("dv", lambda e, dbuf=dbuf:
                                          e.tensor_tensor(
                                              dbuf[:, :].rearrange("p (h n) -> p h n", h=H),
                                              SJb[:, :].rearrange("p (h n) -> p h n", h=H),
                                              FAC[:, :].rearrange("p (h o) -> p h o", o=1)
                                              .broadcast_to([128, H, 128]),
                                              ALU.mult), inc="dv")
                        if t + 1 < NT:
                            P.em("dv", lambda e: e.tensor_tensor(
                                FAC[:, :], FAC[:, :], CDB[:, :], ALU.mult),
                                inc="dv")
                tc = t - 1
                if 0 <= tc < NT:
                    P.wait("dv", cmm_ev[tc], ev_scan)
                    P.em("dv", lambda e, tc=tc: e.tensor_tensor(
                        ob(tc)[:, 0:1024], PA[:, :], ob(tc)[:, 0:1024],
                        ALU.add), inc="dv")
                    cadd_ev[tc] = P.em("dv", lambda e, tc=tc: e.tensor_tensor(
                        ob(tc)[:, 1024:2048], PB[:, :], ob(tc)[:, 1024:2048],
                        ALU.add), inc="dv")
                if t < NT:
                    qbuf = KCH if t % 2 == 0 else KDF
                    dbuf = SB16 if t % 2 == 0 else DT
                    P.wait("pe", ds16_ev[t], qp2_ev[t], cadd_ev.get(t - 1))
                    for h in range(H):
                        ps = PA if h < 8 else PB
                        off = (h % 8) * 128
                        hsl = slice(h * 128, (h + 1) * 128)
                        cmm_ev[t] = P.em("pe", lambda e, ps=ps, off=off,
                                         hsl=hsl, qbuf=qbuf, dbuf=dbuf:
                                         e.matmul(
                                             ps[:, off:off + 128], qbuf[:, hsl],
                                             dbuf[:, hsl], start=True,
                                             stop=True),
                                         inc="pe" if h == H - 1 else None)
            if t < NT:
                P.wait("pe", sig_ev.get(t - 1))
                for c4 in range(4):
                    ev_g = P.em("pe", lambda e, t=t, c4=c4: e.matmul(
                        PC[:, c4 * 512:(c4 + 1) * 512],
                        G1[:, t * 128:(t + 1) * 128],
                        WG2[:, c4 * 512:(c4 + 1) * 512],
                        start=True, stop=True),
                        inc="pe" if c4 == 3 else None)
                sbuf_g = LS if t % 2 == 0 else VC
                P.wait("ac", ev_g, ev_scan)
                sig_ev[t] = P.em("ac", lambda e, sbuf_g=sbuf_g: e.activation(
                    sbuf_g[:, :], PC[:, :], AF.Sigmoid), inc="ac")
            tp_ = t - 1
            if 0 <= tp_ < NT:
                sbuf_gp = LS if tp_ % 2 == 0 else VC
                P.wait("dv", sig_ev[tp_])
                obg_ev[tp_] = P.em("dv", lambda e, tp_=tp_, sbuf_gp=sbuf_gp:
                                   e.tensor_tensor(ob(tp_), ob(tp_),
                                                   sbuf_gp[:, :], ALU.mult),
                                   inc="dv")
                P.wait("ac", obg_ev[tp_])
                ev_sq = P.em("ac", lambda e, tp_=tp_: e.activation(
                    TMPF[:, :], ob(tp_), AF.Square), inc="ac")
                P.wait("dv", ev_sq)
                P.em("dv", lambda e, tp_=tp_: e.tensor_reduce(
                    STT[:, tp_ * 16:(tp_ + 1) * 16].rearrange(
                        "p (h o) -> p h o", h=H),
                    TMPF[:, :].rearrange("p (h n) -> p h n", h=H), AX, ALU.add),
                    inc="dv")
        ev_rms = P.ev("dv")
        if prefix:
            P.wait("sy", cadd_ev[NT - 1])
            ev_rel = emit_reload()
        P.wait("ac", ev_rms)
        P.em("ac", lambda e: e.activation(RSTD[:, :], STT[:, :], AF.Ln,
                                          scale=1.0 / HD, bias=EPSC[:, 0:1]), inc="ac")
        ev_rstd = P.em("ac", lambda e: e.activation(
            RSTD[:, :], RSTD[:, :], AF.Exp, scale=-0.5), inc="ac")
        P.wait("dv", ev_rstd, ev_nw2)
        on_ev = {}
        tr_ev = {}
        cfm_ev = {}
        wob_ev = {}
        RTb16 = RT[:, 0:64].bitcast(BF16)
        ev_rb16 = P.em("dv", lambda e: e.tensor_copy(RTb16, RSTD[:, :]),
                       inc="dv")
        for t in range(NT + 1):
            if t < NT:
                buf = LS if t % 2 == 0 else VC
                P.wait("dv", tr_ev.get(t - 2), sig_ev.get(NT - 1) if t <= 1 else None)
                P.em("dv", lambda e, t=t, buf=buf: e.tensor_tensor(
                    buf[:, :], ob(t), NW[:, :], ALU.mult), inc="dv")
                P.em("dv", lambda e, t=t, buf=buf: e.tensor_tensor(
                    buf[:, :].rearrange("p (h n) -> p h n", h=H),
                    buf[:, :].rearrange("p (h n) -> p h n", h=H),
                    RTb16[:, t * 16:(t + 1) * 16]
                    .rearrange("p (h o) -> p h o", o=1)
                    .broadcast_to([128, H, 128]), ALU.mult), inc="dv")
                on_ev[t] = P.ev("dv")
                P.wait("po", on_ev[t], *ev_dmp)
                wob_ev[t] = P.em("po", lambda e, t=t: e.dma_start(
                    out=B[:, t * 2048:(t + 1) * 2048].rearrange(
                        "p (g d) -> p g d", g=1),
                    in_=wo_in[t * 128:(t + 1) * 128, :].rearrange(
                        "(g p) d -> p g d", p=128)), inc="swb", by=16)
            tp_ = t - 1
            if 0 <= tp_ < NT:
                buf = LS if tp_ % 2 == 0 else VC
                half = (tp_ % 2) * 2048
                P.wait("pe", on_ev[tp_], cfm_ev.get(tp_ - 2))
                for k in range(KC):
                    tr_ev[tp_] = P.em("pe", lambda e, k=k, buf=buf, half=half:
                                      e.transpose(
                                          PCb[:, half + k * 128:half + (k + 1) * 128],
                                          buf[:, k * 128:(k + 1) * 128],
                                          IDT[:, :]), inc="pe" if k == KC - 1 else None)
                P.wait("ac", tr_ev[tp_])
                cfm_ev[tp_] = P.em("ac", lambda e, tp_=tp_, half=half: e.copy(
                    C3[:, :, tp_ * 128:(tp_ + 1) * 128],
                    PCb[:, half:half + 2048].rearrange("p (k n) -> p k n", k=KC)),
                    inc="ac")
        ev_onorm = cfm_ev[NT - 1]

        ev_wob = ("swb", 8 * 16)

        wo_war = {0: None, 1: None}

        def wo_group(ks, ev_ready):
            for t in range(NT):
                even = (t % 2 == 0)
                pss = [(PA, 0), (PA, 512), (PB, 0), (PB, 512)] if even else \
                      [(PC, 0), (PC, 512), (PC, 1024), (PC, 1536)]
                P.wait("pe", ev_ready, wo_war[t % 2])
                nk = len(ks)
                for ki, (k, rhs_fn) in enumerate(ks):
                    for c4, (ps, off) in enumerate(pss):
                        ev_mm = P.em("pe", lambda e, k=k, t=t, ps=ps, off=off,
                                     c4=c4, rhs_fn=rhs_fn, ki=ki: e.matmul(
                                         ps[:, off:off + 512],
                                         C3[:, k, t * 128:(t + 1) * 128],
                                         rhs_fn(c4), start=(ki == 0),
                                         stop=(ki == nk - 1)),
                                     inc="pe" if (ki == nk - 1 and c4 == 3) else None)
                P.wait("dv", ev_mm)
                for c4, (ps, off) in enumerate(pss):
                    P.em("dv", lambda e, t=t, ps=ps, off=off, c4=c4: e.tensor_tensor(
                        xs(t)[:, c4 * 512:(c4 + 1) * 512],
                        ps[:, off:off + 512],
                        xs(t)[:, c4 * 512:(c4 + 1) * 512], ALU.add), inc="dv")
                wo_war[t % 2] = P.ev("dv")

        P.wait("dv", ev_rel)
        P.wait("pe", sig_ev[NT - 1], ev_rms, ev_onorm)
        B8 = B[:, :].rearrange("p (g d) -> p g d", g=8)
        grp0 = [(k, (lambda k_: (lambda c4: B8[:, k_, c4 * 512:(c4 + 1) * 512]))(k))
                for k in range(8)]
        wo_group(grp0, ev_wob)
        for grp_ks in ((8, 9, 10), (11, 12, 13), (14, 15)):
            ks = []
            for k in grp_ks:
                s, ev_l = load_wslot(
                    wo_in[k * 128:(k + 1) * 128, :].rearrange(
                        "(g p) d -> p g d", p=128))
                P.wait("pe", ev_l)
                ks.append((k, (lambda s_: (lambda c4: wslot[s_][:, c4 * 512:(c4 + 1) * 512]))(s)))
                ring_owner[s] = None
            wo_group(ks, None)
            last_pe = P.ev("pe")
            for s in range(3):
                if ring_owner[s] is None:
                    ring_owner[s] = last_pe
        ev_x1 = P.ev("dv")

        if "d_x1" in dump_t:
            P.wait("sy", ev_x1)
            ev_dx1 = P.em("sy", lambda e: e.dma_start(
                out=dump_t["d_x1"], in_=A[:, :]), inc="dxs", by=16)
            P.wait("dv", ev_dx1)

        n_y_stores = 0
        if upto == "attn":
            P.wait("sy", ev_x1)
            for t in range(NT):
                P.em("sy", lambda e, t=t: e.dma_start(
                    out=y_out[t * 128:(t + 1) * 128, :], in_=xs(t)),
                    inc="st", by=16)
                n_y_stores += 1
            P.wait("sy", P.ev("st"), P.ev("dst"), P.ev("slc"), P.ev("dxs"))
        else:
            P.wait("ac", ev_x1)
            P.wait("dv", ev_x1)
            ln2_read, ev_ln2 = emit_ln(xs, B3)

            P.wait("pe", ev_ln2, ev_rwt)
            for k in range(KC):
                P.em("pe", lambda e, k=k: e.matmul(
                    PA[0:8, 0:512], RWT[:, k * 8:(k + 1) * 8],
                    fmb(k, slice(0, 512)), start=(k == 0), stop=(k == KC - 1)))
                ev_rl = P.em("pe", lambda e, k=k: e.matmul(
                    PA[0:8, 512:1024], RWT[:, k * 8:(k + 1) * 8],
                    fmb(k, slice(512, 1024)), start=(k == 0), stop=(k == KC - 1)),
                    inc="pe" if k == KC - 1 else None)
            P.wait("dv", ev_rl)
            ev_rcp = P.em("dv", lambda e: e.tensor_copy(LS[0:8, 0:1024], PA[0:8, :]),
                          inc="dv")
            P.wait("pe", ev_rcp)
            for tt in range(NT):
                ev_rtr = P.em("pe", lambda e, tt=tt: e.transpose(
                    PCb[:, tt * 8:(tt + 1) * 8],
                    LS[0:8, tt * 128:(tt + 1) * 128], IDT[0:8, 0:8]),
                    inc="pe" if tt == NT - 1 else None)
            P.wait("dv", ev_rtr)
            ev_rlg = P.em("dv", lambda e: e.tensor_copy(RT[:, 0:64], PCb[:, 0:64]),
                          inc="dv")

            for tt in range(NT):
                lg = RT[:, tt * 8:(tt + 1) * 8]
                ex = RT[:, 64 + tt * 8:64 + (tt + 1) * 8]
                g2 = RT[:, 168 + tt * 8:168 + (tt + 1) * 8]
                nm = RT[:, 128 + tt:128 + tt + 1]
                sm_ = RT[:, 136 + tt:136 + tt + 1]
                m1 = RT[:, 144 + tt:144 + tt + 1]
                m2 = RT[:, 152 + tt:152 + tt + 1]
                rd = RT[:, 160 + tt:160 + tt + 1]
                P.em("dv", lambda e, lg=lg, nm=nm: e.reduce_max(
                    nm, lg, AX, negate=True), inc="dv")
                ev1 = P.ev("dv")
                P.wait("ac", ev1)
                ev2 = P.em("ac", lambda e, lg=lg, ex=ex, nm=nm: e.activation(
                    ex, lg, AF.Exp, bias=nm), inc="ac")
                P.wait("dv", ev2)
                P.em("dv", lambda e, ex=ex, sm_=sm_: e.reduce_sum(sm_, ex, AX),
                     inc="dv")
                P.em("dv", lambda e, sm_=sm_: e.reciprocal(sm_, sm_), inc="dv")
                P.em("dv", lambda e, ex=ex, sm_=sm_: e.tensor_scalar(
                    ex, ex, sm_, None, ALU.mult), inc="dv")
                P.em("dv", lambda e, ex=ex, m1=m1: e.reduce_max(m1, ex, AX),
                     inc="dv")
                P.em("dv", lambda e, ex=ex, m1=m1, g2=g2: e.tensor_scalar(
                    g2, ex, m1, None, ALU.is_ge), inc="dv")
                P.em("dv", lambda e, ex=ex, g2=g2: e.scalar_tensor_tensor(
                    g2, g2, -1e30, ex, ALU.mult, ALU.add), inc="dv")
                P.em("dv", lambda e, g2=g2, m2=m2: e.reduce_max(m2, g2, AX),
                     inc="dv")
                P.em("dv", lambda e, m1=m1, m2=m2, rd=rd: e.tensor_tensor(
                    rd, m1, m2, ALU.add), inc="dv")
                P.em("dv", lambda e, rd=rd: e.tensor_scalar(
                    rd, rd, 1e-20, None, ALU.add), inc="dv")
                P.em("dv", lambda e, rd=rd: e.reciprocal(rd, rd), inc="dv")
                P.em("dv", lambda e, ex=ex, m2=m2, g2=g2: e.tensor_scalar(
                    g2, ex, m2, None, ALU.is_ge), inc="dv")
                P.em("dv", lambda e, ex=ex, g2=g2: e.tensor_tensor(
                    ex, ex, g2, ALU.mult), inc="dv")
                P.em("dv", lambda e, ex=ex, rd=rd, tt=tt: e.tensor_scalar(
                    G[:, tt * 8:(tt + 1) * 8], ex, rd, None, ALU.mult), inc="dv")
            ev_G = P.ev("dv")

            if "d_G" in dump_t:
                P.wait("sy", ev_G)
                P.em("sy", lambda e: e.dma_start(
                    out=dump_t["d_G"][:, 0:64], in_=G[:, :]), inc="dxs", by=16)

            wd_buf = [SRb[:, 0:5632], SRb[:, 5632:11264]]
            wd_owner = [ev_ln2, ev_ln2]
            wd_ctr = [0]

            def load_wd_quarter(wd_ap, dq):
                b = wd_ctr[0] % 2
                wd_ctr[0] += 1
                P.wait("po", wd_owner[b])
                src = wd_ap.rearrange("(g p) d -> p g d", p=128)[:, :, dq * 512:(dq + 1) * 512]
                ev = P.em("po", lambda e, b=b, src=src: e.dma_start(
                    out=wd_buf[b][:, :].rearrange("p (g d) -> p g d", g=NJ),
                    in_=src), inc=f"wd{b}", by=16)
                return b, ev

            experts = []
            for e_i in range(8):
                experts.append((wgE[e_i], wuE[e_i], wdE[e_i], e_i))
            experts.append((sg_in[:, 0:HE], su_in[:, 0:HE], sd_in[0:HE, :], None))
            experts.append((sg_in[:, HE:2 * HE], su_in[:, HE:2 * HE],
                            sd_in[HE:2 * HE, :], None))

            ev_ph2_mm_prev = P.ev("pe")
            ev_evac_prev = ev_G
            ring_n[0] = 5
            ring_owner[3] = ev_ph2_mm_prev
            ring_owner[4] = ev_ph2_mm_prev
            evac_evs = {}
            wd_pending = []
            for (ei, (wg_ap, wu_ap, wd_ap, gsel)) in enumerate(experts):
                wdq0 = load_wd_quarter(wd_ap, 0)
                wd_pending = [wdq0]
                ev_h = {}
                for j in range(NJ):
                    sg_, ev_lg = load_wslot(eblk(wg_ap, j * 128))
                    su_, ev_lu = load_wslot(eblk(wu_ap, j * 128))
                    if j % 2 == 0:
                        psg, psu = PA, PB
                        psg_sl = (0, 1024)
                    else:
                        psg, psu = PC, PC
                    P.wait("pe", ev_lg, ev_lu,
                           ev_h.get(j - 2), ev_evac_prev if j <= 1 else None)
                    for k in range(KC):
                        if j % 2 == 0:
                            g_lo, g_hi = psg[:, 0:512], psg[:, 512:1024]
                            u_lo, u_hi = psu[:, 0:512], psu[:, 512:1024]
                        else:
                            g_lo, g_hi = PC[:, 0:512], PC[:, 512:1024]
                            u_lo, u_hi = PC[:, 1024:1536], PC[:, 1536:2048]
                        P.em("pe", lambda e, sg_=sg_, k=k, g_lo=g_lo: e.matmul(
                            g_lo, wslot[sg_][:, k * 128:(k + 1) * 128],
                            fmb(k, slice(0, 512)),
                            start=(k == 0), stop=(k == KC - 1)))
                        P.em("pe", lambda e, sg_=sg_, k=k, g_hi=g_hi: e.matmul(
                            g_hi, wslot[sg_][:, k * 128:(k + 1) * 128],
                            fmb(k, slice(512, 1024)),
                            start=(k == 0), stop=(k == KC - 1)))
                        P.em("pe", lambda e, su_=su_, k=k, u_lo=u_lo: e.matmul(
                            u_lo, wslot[su_][:, k * 128:(k + 1) * 128],
                            fmb(k, slice(0, 512)),
                            start=(k == 0), stop=(k == KC - 1)))
                        ev_mm = P.em("pe", lambda e, su_=su_, k=k, u_hi=u_hi:
                                     e.matmul(
                                         u_hi, wslot[su_][:, k * 128:(k + 1) * 128],
                                         fmb(k, slice(512, 1024)),
                                         start=(k == 0), stop=(k == KC - 1)),
                                     inc="pe" if k == KC - 1 else None)
                    ring_owner[sg_] = ev_mm
                    ring_owner[su_] = ev_mm
                    P.wait("ac", ev_mm, ev_h.get(j - 1))
                    if j % 2 == 0:
                        hg_ap, hu_ap = PA[:, :], PB[:, :]
                    else:
                        hg_ap, hu_ap = PC[:, 0:1024], PC[:, 1024:2048]
                    ev_si = P.em("ac", lambda e, hg_ap=hg_ap: e.activation(
                        LS[:, 0:1024], hg_ap, AF.Silu), inc="ac")
                    P.wait("dv", ev_si, ev_ph2_mm_prev if j == 0 else None)
                    ev_h[j] = P.em("dv", lambda e, j=j, hu_ap=hu_ap: e.tensor_tensor(
                        C[:, j * 1024:(j + 1) * 1024], LS[:, 0:1024], hu_ap,
                        ALU.mult), inc="dv")
                ev_ph1 = ev_h[NJ - 1]
                ev_ph2_mm = None
                for dq in range(4):
                    bsel, ev_wd = wd_pending[-1]
                    if dq < 3:
                        wd_pending.append(load_wd_quarter(wd_ap, dq + 1))
                    for t in range(NT):
                        qsl = t % 4
                        ps, off = [(PA, 0), (PA, 512), (PB, 0), (PB, 512)][qsl]
                        P.wait("pe", ev_wd, ev_ph1 if (dq == 0 and t == 0) else None,
                               evac_evs.get((dq, t - 4) if t >= 4 else (dq - 1, t + 4)))
                        for j in range(NJ):
                            ev_mm2 = P.em("pe", lambda e, j=j, t=t, ps=ps, off=off,
                                          bsel=bsel: e.matmul(
                                              ps[:, off:off + 512],
                                              C[:, j * 1024 + t * 128:j * 1024 + (t + 1) * 128],
                                              wd_buf[bsel][:, :].rearrange(
                                                  "p (g d) -> p g d", g=NJ)[:, j, :],
                                              start=(j == 0), stop=(j == NJ - 1)),
                                          inc="pe" if j == NJ - 1 else None)
                        ev_ph2_mm = ev_mm2
                        P.wait("dv", ev_mm2)
                        if gsel is None:
                            evv = P.em("dv", lambda e, t=t, dq=dq, ps=ps, off=off:
                                       e.tensor_tensor(
                                           xs(t)[:, dq * 512:(dq + 1) * 512],
                                           ps[:, off:off + 512],
                                           xs(t)[:, dq * 512:(dq + 1) * 512],
                                           ALU.add), inc="dv")
                        else:
                            evv = P.em("dv", lambda e, t=t, dq=dq, ps=ps, off=off,
                                       gsel=gsel: e.scalar_tensor_tensor(
                                           xs(t)[:, dq * 512:(dq + 1) * 512],
                                           ps[:, off:off + 512],
                                           G[:, t * 8 + gsel:t * 8 + gsel + 1],
                                           xs(t)[:, dq * 512:(dq + 1) * 512],
                                           ALU.mult, ALU.add), inc="dv")
                        evac_evs[(dq, t)] = evv
                        if ei == len(experts) - 1 and dq == 3:
                            P.wait("sy", evv)
                            P.em("sy", lambda e, t=t: e.dma_start(
                                out=y_out[t * 128:(t + 1) * 128, :], in_=xs(t)),
                                inc="st", by=16)
                            n_y_stores += 1
                    wd_owner[bsel] = ev_ph2_mm
                ev_ph2_mm_prev = ev_ph2_mm
                ev_evac_prev = evac_evs[(3, NT - 1)]

            P.wait("sy", P.ev("st"), P.ev("dst"), P.ev("slc"), P.ev("dxs"))

        with nc.Block() as block:
            @block.sync
            def _(eng):
                for th in P.thunks["sy"]:
                    th(eng)

            @block.gpsimd
            def _(eng):
                for th in P.thunks["po"]:
                    th(eng)

            @block.tensor
            def _(eng):
                for th in P.thunks["pe"]:
                    th(eng)

            @block.vector
            def _(eng):
                for th in P.thunks["dv"]:
                    th(eng)

            @block.scalar
            def _(eng):
                for th in P.thunks["ac"]:
                    th(eng)

    return nc


_CACHE = {}


def _get_nc(upto="full", dumps=(), prefix=False):
    key = (upto, tuple(sorted(dumps)), prefix)
    if key not in _CACHE:
        _CACHE[key] = build(upto, dumps, prefix)
    return _CACHE[key]


def _mk_inmaps(inputs, s_all_list):
    x = np.ascontiguousarray(inputs["L0_attn_x"].reshape(8192, D))
    ld = np.ascontiguousarray(inputs["L0_attn_log_decay"].reshape(1, H))
    nw = np.ascontiguousarray(inputs["L0_attn_norm_w"].reshape(1, D))
    shared = {
        "ldec": ld, "nw": nw,
        "rw": inputs["L0_ffn_router_w"],
        "wq": inputs["L0_attn_Wq"], "wk": inputs["L0_attn_Wk"],
        "wv": inputs["L0_attn_Wv"], "wo": inputs["L0_attn_Wo"],
        "wg1": inputs["L0_attn_Wg1"], "wg2": inputs["L0_attn_Wg2"],
        "wgE": inputs["L0_ffn_wg"], "wuE": inputs["L0_ffn_wu"],
        "wdE": inputs["L0_ffn_wd"],
        "sg": inputs["L0_ffn_sg"], "su": inputs["L0_ffn_su"],
        "sd": inputs["L0_ffn_sd"],
    }
    shared = {k: np.ascontiguousarray(v) for k, v in shared.items()}
    maps = []
    for i in range(NCORES):
        pce = np.full((1, NCORES), 1e9, dtype=np.float32)
        for j in range(i):
            pce[0, j] = 1024.0 * (i - 1 - j)
        m = dict(shared)
        m["x"] = np.ascontiguousarray(x[i * T:(i + 1) * T])
        m["pce"] = pce
        m["s_all"] = s_all_list[i]
        maps.append(m)
    return maps


def _mk_inmaps_prefix(inputs, loopback=False):
    zero_s = np.zeros((NCORES * HD, D), dtype=np.float32)
    maps = _mk_inmaps(inputs, [zero_s] * NCORES)
    x = np.ascontiguousarray(inputs["L0_attn_x"].reshape(8192, D))
    ld = inputs["L0_attn_log_decay"].reshape(H)
    wk = inputs["L0_attn_Wk"]
    wv = inputs["L0_attn_Wv"]
    for i in range(NCORES):
        p = i // 2
        half = i % 2
        npre_real = p * 2048
        xpre = np.zeros((NPRE, D), dtype=np.float32)
        if npre_real:
            xpre[NPRE - npre_real:] = x[0:npre_real]
        m = maps[i]
        m["xpre"] = xpre
        m["wkh"] = np.ascontiguousarray(wk[:, half * 1024:(half + 1) * 1024])
        m["wvh"] = np.ascontiguousarray(wv[:, half * 1024:(half + 1) * 1024])
        m["ldh"] = np.ascontiguousarray(ld[half * 8:(half + 1) * 8].reshape(1, 8))
        if half == 0:
            m["mg"] = np.array([[0.0, 1e9, 1e9]], dtype=np.float32)
        else:
            m["mg"] = np.array([[1024.0, 0.0, 1024.0]], dtype=np.float32)
        m["slot"] = np.array([[half]], dtype=np.int32)
        m["pslot"] = np.array([[half if loopback else 1 - half]], dtype=np.int32)
    return maps


def kernel(**inputs):
    nc = _get_nc("full", prefix=True)
    maps = _mk_inmaps_prefix(inputs)
    res = run_bass_kernel_spmd(nc, maps, core_ids=list(range(NCORES)))
    y = np.concatenate([res.results[i]["y"] for i in range(NCORES)], axis=0)
    return y.reshape(1, 8192, D).astype(np.float32)


def kernel_2launch(**inputs):
    nc = _get_nc("full")
    zero_s = np.zeros((NCORES * HD, D), dtype=np.float32)
    maps1 = _mk_inmaps(inputs, [zero_s] * NCORES)
    res1 = run_bass_kernel_spmd(nc, maps1, core_ids=list(range(NCORES)))
    s_all = np.concatenate([res1.results[i]["sloc"] for i in range(NCORES)], axis=0)
    s_all = np.ascontiguousarray(s_all)
    maps2 = _mk_inmaps(inputs, [s_all] * NCORES)
    res2 = run_bass_kernel_spmd(nc, maps2, core_ids=list(range(NCORES)))
    y = np.concatenate([res2.results[i]["y"] for i in range(NCORES)], axis=0)
    return y.reshape(1, 8192, D).astype(np.float32)
